# version 01
# baseline (speedup 1.0000x reference)
import os
import numpy as np
from contextlib import ExitStack

import concourse.bass as bass
import concourse.mybir as mybir
from concourse.bass_utils import run_bass_kernel_spmd

F32 = mybir.dt.float32
BF16 = mybir.dt.bfloat16
AF = mybir.ActivationFunctionType
OP = mybir.AluOpType
AXX = mybir.AxisListType.X

B, T, D = 2, 8192, 1024
DIL = 512
NST, DCONV, RNK = 16, 4, 64
H, DH, DLAT = 16, 64, 256
NE = 16
TR = 2048
NTT = TR // 128
MCH = 1024
NCH = T // MCH
GROUPS = [[0, 1, 2, 3], [4, 5, 6, 7]]
LNEPS = 1e-5
ENGS = ("sync", "tensor", "vector", "scalar", "gpsimd")


class Prog:
    def __init__(self, nc):
        self.nc = nc
        self.ops = []
        self.sems = {}
        self.counts = {}
        self.pool = []
        self.closed = set()
        self.dma_tags = set()
        self.bases = {}
        self.pending_fence = {e: [] for e in ENGS}

    def sem(self, name):
        if name not in self.sems:
            assert name not in self.closed, f"sem {name} used after release"
            if self.pool and name not in self.dma_tags:
                handle, base = self.pool.pop()
            else:
                handle, base = self.nc.semaphore(name).__enter__(), 0
            self.sems[name] = handle
            self.counts[name] = base
            self.bases[name] = base
        return self.sems[name]

    def release(self, prefixes):
        for name in list(self.sems):
            if name in self.closed:
                continue
            if any(name.startswith(p) for p in prefixes):
                if name in self.dma_tags:
                    continue
                self.pool.append((self.sems[name], self.counts[name]))
                self.closed.add(name)

    def op(self, eng, emit, waits=(), sig=None, dma=False):
        sigs = []
        if sig:
            if isinstance(sig, str):
                sig = [sig]
            for s in sig:
                if isinstance(s, str):
                    s = (s, 16 if dma else 1)
                assert s[0] not in self.closed, f"sig on released sem {s[0]}"
                if dma:
                    assert s[0] not in self.bases or self.bases[s[0]] == 0, \
                        f"dma tag {s[0]} grabbed a pooled sem; pre-declare it"
                    self.dma_tags.add(s[0])
                self.sem(s[0])
                self.counts[s[0]] += s[1]
                sigs.append(s)
        w = [x for x in waits if x is not None and x[1] > 0]
        if self.pending_fence[eng]:
            w = self.pending_fence[eng] + w
            self.pending_fence[eng] = []
        for x in w:
            self.sem(x[0])
        self.ops.append((eng, emit, w, sigs))

    def ev(self, semname):
        self.sem(semname)
        return (semname, self.counts[semname])

    def n(self, semname):
        self.sem(semname)
        return self.counts[semname]

    def w(self, semname, v):
        self.sem(semname)
        return (semname, self.bases[semname] + v)

    def fence(self, evs):
        evs = [x for x in evs if x is not None and x[1] > 0]
        for e in ENGS:
            self.pending_fence[e] = list(evs)

    def emit_all(self, block):
        by = {e: [] for e in ENGS}
        for (eng, emit, waits, sigs) in self.ops:
            by[eng].append((emit, waits, sigs))

        chain = {e: self.nc.semaphore(f"chain_{e}").__enter__()
                 for e in ("vector", "scalar", "gpsimd")}
        _NOCHAIN = ("dma_start", "collective_compute", "wait_ge",
                    "indirect_dma_start")

        class ChainedEngine:

            def __init__(self, eng, csem):
                self._e = eng
                self._c = csem
                self._n = 0
                self._last = None
                self._point = None

            def _flush_last(self):
                if self._last is not None:
                    self._n += 1
                    self._last.then_inc(self._c, 1)
                    self._point = (self._c, self._n)
                    self._last = None

            def __getattr__(self, name):
                f = getattr(self._e, name)
                if not callable(f) or name.startswith("_") or name in _NOCHAIN:
                    return f
                proxy = self

                def wrap(*a, **k):
                    proxy._flush_last()
                    if proxy._point is not None:
                        proxy._e.wait_ge(proxy._point[0], proxy._point[1])
                    ins = f(*a, **k)
                    proxy._last = ins
                    return ins
                return wrap

        def runner(name, lst):
            csem = chain.get(name)

            def run(engine):
                eng = ChainedEngine(engine, csem) if csem is not None else engine
                run_counts = {}
                for (emit, waits, sigs) in lst:
                    seen = set()
                    for (s, v) in waits:
                        if (s, v) in seen:
                            continue
                        seen.add((s, v))
                        engine.wait_ge(self.sems[s], v)
                    ins = emit(eng)
                    if csem is None:
                        for (s, k) in sigs:
                            ins.then_inc(self.sems[s], k)
                    elif sigs:
                        assert len(sigs) == 1, "chained engines: one sig per op"
                        s, k = sigs[0]
                        if s not in run_counts:
                            run_counts[s] = self.bases.get(s, 0)
                        run_counts[s] += k
                        if eng._last is ins:
                            ins.then_inc(self.sems[s], k)
                            eng._point = (self.sems[s], run_counts[s])
                            eng._last = None
                        else:
                            ins.then_inc(self.sems[s], k)
            return run

        for name in ENGS:
            if by[name]:
                getattr(block, name)(runner(name, by[name]))


def build_program(dbg, maxstage=4):
    nc = bass.Bass("TRN2", target_bir_lowering=False)
    es = ExitStack()

    def inp(name, shape, dt=F32):
        return nc.dram_tensor(name, list(shape), dt, kind="ExternalInput").ap()

    x_b = inp("x_b", (T, D))
    x_tr = inp("x_tr", (TR, D))
    in_w_l = inp("in_w_l", (2, D, 2 * DIL))
    conv_w_l = inp("conv_w_l", (128, 8, DCONV))
    conv_b_l = inp("conv_b_l", (128, 8))
    xp_w_l = inp("xp_w_l", (2, DIL, 96))
    dt_w_l = inp("dt_w_l", (2, RNK, DIL))
    dt_b_l = inp("dt_b_l", (128, 8))
    A_log_l = inp("A_log_l", (128, 8, NST))
    D_l = inp("D_l", (128, 8))
    out_wT_l = inp("out_wT_l", (2, D, DIL))
    proj_w_d = inp("proj_w_d", (2, D, D))
    proj_b = inp("proj_b", (1, D))
    Wq_i = inp("Wq", (D, D))
    bq_i = inp("bq", (1, D))
    Wdown_i = inp("Wdown", (D, DLAT))
    bdown_i = inp("bdown", (1, DLAT))
    Wup_k_i = inp("Wup_k", (DLAT, D))
    Wup_v_i = inp("Wup_v", (DLAT, D))
    bup_k_i = inp("bup_k", (1, D))
    bup_v_i = inp("bup_v", (1, D))
    Wout_i = inp("Wout", (D, D))
    bout_i = inp("bout", (1, D))
    centT_i = inp("centT", (D, NE))
    eb_i = inp("eb", (1, NE))
    WrT_i = inp("WrT", (NE, D, D))
    br_i = inp("br", (NE, D))
    WsT_i = inp("WsT", (2, D, D))
    bs_i = inp("bs", (2, D))
    identity_i = inp("identity", (128, 128), F32)
    ropeq_tok = inp("ropeq_tok", (TR, 2 * DIL))
    bdmask_i = inp("bdmask", (128, 128))

    out_t = nc.dram_tensor("out", [TR, D], F32, kind="ExternalOutput").ap()
    dbg_t = [nc.dram_tensor(f"dbg{i}", [TR, D], F32, kind="ExternalOutput").ap()
             for i in range(4)] if dbg else None

    dram = nc.dram_tensor
    dbc_in = dram("dbc_in", [2, 96, T], F32)
    dbc_out = dram("dbc_out", [2, 96, T], F32)
    bc_rows = dram("bc_rows", [2, 32, T], BF16)
    xc_sp = dram("xc_sp", [2, 4, 128, T], BF16)
    sz_sp = dram("sz_sp", [2, 4, 128, T], BF16)
    yf_sp = dram("yf_sp", [4, 128, T], BF16)
    mout_in = dram("mout_in", [T, D], F32)
    mout_rs = dram("mout_rs", [TR, D], F32)
    q_sp = dram("q_sp", [TR, D], BF16)
    k_sp = dram("k_sp", [TR, D], BF16)
    v_sp = dram("v_sp", [TR, D], BF16)
    att_sp = dram("att_sp", [TR, D], BF16)
    wr_bf_sp = dram("wr_bf_sp", [NE, 128, 8, D], BF16)
    wsh_sp = dram("wsh_sp", [128, 8, D], BF16)
    br_hb = dram("br_hb", [NE + 1, D], BF16)
    wo_sp = dram("wo_sp", [128, 8, D], BF16)
    rk_bf_sp = dram("rk_bf_sp", [2, 128, T], BF16)
    mb_hb = dram("mb_hb", [1, 1], F32)

    def sb(name, shape, dt=F32, stack=None):
        return (stack or es).enter_context(nc.sbuf_tensor(name, list(shape), dt))

    id_bf = sb("id_bf", [128, 128], F32)
    ones_bf = sb("ones_bf", [128, 128], BF16)
    lnst = sb("lnst", [128, 12])
    lnmv = sb("lnmv", [128, 4])
    epsc = sb("epsc", [128, 2])

    PSA = es.enter_context(nc.psum_tensor("PSA", [128, 1024], F32))
    PSB = es.enter_context(nc.psum_tensor("PSB", [128, 1024], F32))
    PSC = es.enter_context(nc.psum_tensor("PSC", [128, 1024], F32))
    PSD = es.enter_context(nc.psum_tensor("PSD", [128, 512], F32))
    PSE = es.enter_context(nc.psum_tensor("PSE", [128, 512], F32))
    SLOT6 = [PSA[:, 0:512], PSA[:, 512:1024], PSB[:, 0:512], PSB[:, 512:1024],
             PSC[:, 0:512], PSC[:, 512:1024]]
    TPS = [PSA[:, 0:128], PSA[:, 512:640], PSB[:, 0:128], PSB[:, 512:640],
           PSC[:, 0:128], PSC[:, 512:640], PSD[:, 0:128], PSE[:, 0:128]]

    block = es.enter_context(nc.Block())
    P = Prog(nc)
    P.dma_tags.update(["s3b_od0", "s3b_od1", "s3a_ysp", "s1_sp", "s1_zsp",
                       "a_qsp0", "a_qsp1", "a_qsp2", "a_qsp3", "a_ksp0", "a_ksp1", "a_ksp2",
                       "a_ksp3", "a_asp0", "a_asp1", "a_wos", "a_ld0", "a_ld1",
                       "a_rw", "a_or0", "a_or1", "s3x_b0", "s3x_b1", "s3x_c0",
                       "s3x_c1", "me_shl", "me_brl0", "me_brl1", "a_wl", "a_bl0", "a_bl1",
                       "a_bsp0", "a_bsp1", "a_vsp0", "a_vsp1", "a_bm", "a_vl0", "a_vl1", "m0_ws0", "m0_ws1", "mw_sp", "out_d", "dbg_d"])

    def _init_consts(e):
        e.memset(epsc[:, 0:1], LNEPS)
        return e.memset(epsc[:, 1:2], 1e-30)
    P.op("vector", _init_consts, sig="c_init")
    e_const = P.ev("c_init")

    def dmaop(eng, out_ap, in_ap, sig, waits=()):
        P.op(eng, lambda e, o=out_ap, i=in_ap: e.dma_start(out=o, in_=i),
             waits=waits, sig=sig, dma=True)

    def ln_tile(tag, src_ap, dst_ap, dst_waits=()):
        sv, ss = tag + "_v", tag + "_s"

        def v1(e, src=src_ap):
            e.bn_stats(lnst[:, 0:6], src[:, 0:512])
            e.bn_stats(lnst[:, 6:12], src[:, 512:1024])
            e.bn_aggr(lnmv[:, 0:2], lnst[:, 0:12])
            return e.tensor_copy(lnmv[:, 2:3], lnmv[:, 1:2])
        P.op("vector", v1, sig=sv)

        def s1(e):
            e.activation(lnmv[:, 2:3], lnmv[:, 2:3], AF.Ln, bias=epsc[:, 0:1])
            return e.activation(lnmv[:, 2:3], lnmv[:, 2:3], AF.Exp, scale=-0.5)
        P.op("scalar", s1, waits=[P.ev(sv), e_const], sig=ss)
        P.op("vector",
             lambda e, src=src_ap, dst=dst_ap: e.tensor_scalar(
                 dst, src, lnmv[:, 0:1], lnmv[:, 2:3], OP.subtract, OP.mult),
             waits=[P.ev(ss)] + list(dst_waits), sig=sv)
        return P.ev(sv)

    es_m = ExitStack()
    es_m1 = ExitStack()

    es_s0 = ExitStack()
    convw = sb("convw", [128, 8, DCONV], stack=es_m)
    convb = sb("convb", [128, 8], stack=es_m)
    dtb = sb("dtb", [128, 8], stack=es_m)
    Dp = sb("Dp", [128, 8], stack=es_m)
    A_f = sb("A_f", [128, 8, NST], stack=es_m)
    xpw_bf = sb("xpw_bf", [128, 8, 96], BF16, stack=es_m)
    dtw_bf = sb("dtw_bf", [128, DIL], BF16, stack=es_m)
    weff = sb("weff", [128, 8, D], BF16, stack=es_m)
    dbcT = sb("dbcT", [128, 2, T], stack=es_m1)

    inw_bf = sb("inw_bf", [128, 16, D], BF16, stack=es_m1)
    stg_big = sb("stg_big", [128, 8, D], stack=es_s0)
    stg_bf = sb("stg_bf", [128, 8, D], BF16, stack=es_s0)

    dmaop("sync", id_bf[:], identity_i, "s0_ld")
    dmaop("sync", convw[:], conv_w_l, "s0_ld")
    dmaop("sync", convb[:], conv_b_l, "s0_ld")
    dmaop("sync", dtb[:], dt_b_l, "s0_ld")
    dmaop("sync", Dp[:], D_l, "s0_ld")
    dmaop("sync", A_f[:], A_log_l, "s0_ld")
    e_s0 = P.ev("s0_ld")
    P.op("vector", lambda e: e.memset(ones_bf[:], 1.0), sig="s0_v")
    P.op("scalar", lambda e: e.activation(A_f[:], A_f[:], AF.Exp),
         waits=[e_s0], sig="s0_s")
    P.op("vector", lambda e: e.tensor_scalar_mul(A_f[:], A_f[:], -1.0),
         waits=[P.ev("s0_s")], sig="s0_v")
    e_A = P.ev("s0_v")

    stg_flat = stg_big[:].rearrange("p a m -> p (a m)")

    def load_convert(dst_bf_ap, hbm_ap, ngrp, width):
        v = stg_flat[:, 0:ngrp * width].rearrange("p (g w) -> p g w", g=ngrp)
        dmaop("sync", v, hbm_ap, "lc_ld", waits=[P.ev("lc_cv")])
        P.op("vector",
             lambda e, d=dst_bf_ap, v=v: e.tensor_copy(d, v),
             waits=[P.ev("lc_ld")], sig="lc_cv")
        return P.ev("lc_cv")

    inw_r = in_w_l.rearrange("r (kt p) m -> p (r kt) m", p=128)
    load_convert(inw_bf[:, 0:8, :], inw_r[:, 0:8, :], 8, D)
    load_convert(inw_bf[:, 8:16, :], inw_r[:, 8:16, :], 8, D)
    load_convert(xpw_bf[:], xp_w_l.rearrange("r (kt p) m -> p (r kt) m", p=128), 8, 96)
    load_convert(dtw_bf[:].rearrange("p (a m) -> p a m", a=1),
                 dt_w_l.rearrange("r k m -> (r k) m").rearrange("p (a m) -> p a m", a=1),
                 1, DIL)
    e_w0 = P.ev("lc_cv")

    pw_bf = sb("pw_bf", [128, 16, D], BF16, stack=es_s0)
    stgv = stg_bf[:].rearrange("p a m -> p (a m)").rearrange(
        "p (g w) -> p g w", g=16)
    owr = out_wT_l.rearrange("r (kt p) m -> p (r kt) m", p=128)
    e_ow = load_convert(stgv, owr, 16, DIL)
    pwr = proj_w_d.rearrange("r (kt p) m -> p (r kt) m", p=128)
    load_convert(pw_bf[:, 0:8, :], pwr[:, 0:8, :], 8, D)
    e_pw = load_convert(pw_bf[:, 8:16, :], pwr[:, 8:16, :], 8, D)
    gi = 0
    for dir_ in range(2):
        for rt in range(4):
            for nchk in range(2):
                ps = SLOT6[gi % 4]

                def mm(e, dir_=dir_, rt=rt, nchk=nchk, ps=ps):
                    for kt in range(8):
                        ins = e.matmul(ps,
                                       stgv[:, dir_ * 8 + kt, rt * 128:(rt + 1) * 128],
                                       pw_bf[:, dir_ * 8 + kt, nchk * 512:(nchk + 1) * 512],
                                       start=(kt == 0), stop=(kt == 7))
                    return ins
                w = [e_ow, e_pw]
                if gi >= 4:
                    w.append(("weff_e", gi - 3))
                P.op("tensor", mm, waits=w, sig="weff_m")
                P.op("vector",
                     lambda e, dir_=dir_, rt=rt, nchk=nchk, ps=ps: e.tensor_copy(
                         weff[:, dir_ * 4 + rt, nchk * 512:(nchk + 1) * 512], ps),
                     waits=[P.ev("weff_m")], sig="weff_e")
                gi += 1
    e_weff = P.ev("weff_e")
    es_s0.close()

    xstage = sb("xstage", [128, 2, D], stack=es_m1)
    xhat_m = sb("xhat_m", [128, 2, D], stack=es_m1)
    xhat1T = sb("xhat1T", [128, 8, MCH], BF16, stack=es_m1)
    xibufA = sb("xibufA", [128, 8, MCH + 3], BF16, stack=es_m1)
    xibufB = sb("xibufB", [128, 8, MCH + 3], BF16, stack=es_m1)
    cacc = sb("cacc", [128, 2, MCH], stack=es_m1)
    xcc = sb("xcc", [128, 4, MCH], BF16, stack=es_m1)
    szc = sb("szc", [128, 4, MCH], BF16, stack=es_m1)

    P.op("vector", lambda e: e.memset(dbcT[:], 0.0), sig="s1_dbc0")
    e_dbc0 = P.ev("s1_dbc0")
    NXT = MCH // 128

    def conv_dir(c_conv, dir_, base, buf):
        e_prev_sp = P.ev("s1_sp")
        e_prev_xp = P.ev("s1_xpe")
        for dtl in range(4):
            def cv(e, dtl=dtl, base=base, buf=buf, dir_=dir_):
                ins = None
                for k in range(4):
                    wi = k if dir_ == 0 else 3 - k
                    s_ap = buf[:, base + dtl, k:k + MCH]
                    wc = convw[:, dir_ * 4 + dtl, wi:wi + 1]
                    if k == 0:
                        ins = e.tensor_scalar_mul(cacc[:, 0, :], s_ap, wc)
                    else:
                        ins = e.scalar_tensor_tensor(cacc[:, k % 2, :], s_ap, wc,
                                                     cacc[:, (k + 1) % 2, :],
                                                     OP.mult, OP.add)
                return ins
            P.op("vector", cv, waits=[P.ev("s1_xc")], sig="s1_cv")
            P.op("scalar",
                 lambda e, dtl=dtl, dir_=dir_: e.activation(
                     xcc[:, dtl, :], cacc[:, 1, :], AF.Silu,
                     bias=convb[:, dir_ * 4 + dtl:dir_ * 4 + dtl + 1]),
                 waits=[P.ev("s1_cv"), e_prev_sp, e_prev_xp], sig="s1_xc")
            P.op("gpsimd",
                 lambda e, dtl=dtl, dir_=dir_, c_conv=c_conv: e.dma_start(
                     out=xc_sp.ap()[dir_, dtl, :, c_conv * MCH:(c_conv + 1) * MCH],
                     in_=xcc[:, dtl, :]),
                 waits=[P.ev("s1_xc")], sig="s1_sp", dma=True)
        e_xc = P.ev("s1_xc")
        for nchk in range(2):
            ps = [PSD, PSE][nchk]

            def mm(e, dir_=dir_, nchk=nchk, ps=ps):
                for kt in range(4):
                    ins = e.matmul(ps[0:96, :], xpw_bf[:, dir_ * 4 + kt, :],
                                   xcc[:, kt, nchk * 512:(nchk + 1) * 512],
                                   start=(kt == 0), stop=(kt == 3))
                return ins
            P.op("tensor", mm, waits=[e_xc, e_w0, P.ev("s1_xpe")], sig="s1_xp")
            P.op("vector",
                 lambda e, dir_=dir_, nchk=nchk, ps=ps, c_conv=c_conv: e.tensor_add(
                     dbcT[0:96, dir_, c_conv * MCH + nchk * 512:c_conv * MCH + (nchk + 1) * 512],
                     dbcT[0:96, dir_, c_conv * MCH + nchk * 512:c_conv * MCH + (nchk + 1) * 512],
                     ps[0:96, :]),
                 waits=[P.ev("s1_xp"), e_dbc0], sig="s1_xpe")

    slot6_guard = [None] * 6
    for c in range(NCH):
        pA, pP = (c % 2) * 4, ((c + 1) % 2) * 4
        e_prev_evx = P.ev("s1_evx")
        e_prev_evz = P.ev("s1_evz")
        e_prev_xpe2 = P.ev("s1_xpe")
        for i in range(NXT):
            k = c * NXT + i
            w = [e_w0, e_weff] if k < 2 else [("s1_ln_v", 2 * (k - 1))]
            dmaop("sync", xstage[:, k % 2, :],
                  x_b[c * MCH + 128 * i: c * MCH + 128 * (i + 1), :],
                  f"s1_x{k % 2}", waits=w)
            P.op("vector", lambda e: e.engine_nop(),
                 waits=[(f"s1_x{k % 2}", 16 * (k // 2 + 1))])
            dwk = [] if k < 2 else [("s1_tp", 8 * (k - 1))]
            lev = ln_tile("s1_ln", xstage[:, k % 2, :], xhat_m[:, k % 2, :],
                          dst_waits=dwk)
            for j in range(8):
                extra = [lev, e_prev_evx, e_prev_evz, e_prev_xpe2]
                if c == 0 and k == 0:
                    extra.append(e_weff)
                if k > 0:
                    extra.append(("s1_tpe", 8 * k))
                P.op("tensor",
                     lambda e, j=j, k=k: e.transpose(
                         TPS[j], xhat_m[:, k % 2, j * 128:(j + 1) * 128], id_bf[:]),
                     waits=extra, sig="s1_tp")
                P.op("vector",
                     lambda e, j=j, i=i: e.tensor_copy(
                         xhat1T[:, j, i * 128:(i + 1) * 128], TPS[j]),
                     waits=[P.ev("s1_tp")], sig="s1_tpe")
        e_tp = P.ev("s1_tpe")
        gidx = 0
        for dir_ in range(2):
            for mt in range(8):
                for nchk in range(2):
                    ps = SLOT6[gidx % 6]

                    def mm(e, dir_=dir_, mt=mt, nchk=nchk, ps=ps):
                        for kt in range(8):
                            ins = e.matmul(ps,
                                           inw_bf[:, dir_ * 8 + kt, mt * 128:(mt + 1) * 128],
                                           xhat1T[:, kt, nchk * 512:(nchk + 1) * 512],
                                           start=(kt == 0), stop=(kt == 7))
                        return ins
                    w = [e_tp, e_w0]
                    if slot6_guard[gidx % 6] is not None:
                        w.append(slot6_guard[gidx % 6])
                    P.op("tensor", mm, waits=w, sig="s1_mm")
                    if mt < 4:
                        buf = xibufA if dir_ == 0 else xibufB
                        off = 3 if dir_ == 0 else 0
                        P.op("vector",
                             lambda e, buf=buf, off=off, mt=mt, nchk=nchk, ps=ps, pA=pA:
                             e.tensor_copy(
                                 buf[:, pA + mt, off + nchk * 512: off + (nchk + 1) * 512], ps),
                             waits=[P.ev("s1_mm")], sig="s1_evx")
                        slot6_guard[gidx % 6] = P.ev("s1_evx")
                    else:
                        zguard = ("s1_zsp", 64 * (2 * c + dir_))
                        P.op("scalar",
                             lambda e, mt=mt, nchk=nchk, ps=ps:
                             e.activation(szc[:, mt - 4, nchk * 512:(nchk + 1) * 512],
                                          ps, AF.Silu),
                             waits=[P.ev("s1_mm"), zguard], sig="s1_evz")
                        slot6_guard[gidx % 6] = P.ev("s1_evz")
                    gidx += 1
            e_zev = P.ev("s1_evz")
            for g in range(4):
                P.op("gpsimd",
                     lambda e, g=g, c=c, dir_=dir_: e.dma_start(
                         out=sz_sp.ap()[dir_, g, :, c * MCH:(c + 1) * MCH],
                         in_=szc[:, g, :]),
                     waits=[e_zev], sig="s1_zsp", dma=True)
        e_ev = [P.ev("s1_evx"), P.ev("s1_evz")]

        def pads(e, c=c, pA=pA, pP=pP):
            if c == 0:
                ins = e.memset(xibufA[:, pA:pA + 4, 0:3], 0.0)
            else:
                ins = e.tensor_copy(xibufA[:, pA:pA + 4, 0:3],
                                    xibufA[:, pP:pP + 4, MCH:MCH + 3])
            if c > 0:
                ins = e.tensor_copy(xibufB[:, pP:pP + 4, MCH:MCH + 3],
                                    xibufB[:, pA:pA + 4, 0:3])
            return ins
        P.op("vector", pads, waits=e_ev, sig="s1_pad")
        conv_dir(c, 0, pA, xibufA)
        if c > 0:
            conv_dir(c - 1, 1, pP, xibufB)
        if c == NCH - 1:
            P.op("vector",
                 lambda e, pA=pA: e.memset(xibufB[:, pA:pA + 4, MCH:MCH + 3], 0.0),
                 sig="s1_pad")
            conv_dir(NCH - 1, 1, pA, xibufB)

    e_front = [P.ev("s1_xpe"), P.ev("s1_sp"), P.ev("s1_zsp")]

    def finish_any(e_last, src, stacks):
        nblk = max(1, src.shape[-1] // 1024)
        for ti in range(NTT):
            dmaop("sync", out_t[ti * 128:(ti + 1) * 128, :],
                  src[:, (ti % nblk) * 1024:(ti % nblk + 1) * 1024], "out_d",
                  waits=e_last)
        P.op("vector", lambda e: e.engine_nop(), waits=[P.ev("out_d")])
        P.emit_all(block)
        for s in stacks:
            s.close()
        es.close()
        return nc
    if maxstage == "s1":
        return finish_any(e_front, dbcT[:, 0, :], [es_m1, es_m])
    if maxstage == "s1x":
        for ti in range(NTT):
            dmaop("gpsimd", out_t[ti * 128:(ti + 1) * 128, :],
                  xc_sp.ap()[0, ti // 4, :, (ti % 4) * 1024:(ti % 4 + 1) * 1024],
                  "out_d", waits=e_front)
        P.op("vector", lambda e: e.engine_nop(), waits=[P.ev("out_d")])
        P.emit_all(block)
        es_m1.close(); es_m.close(); es.close()
        return nc
    for dir_ in range(2):
        P.op("gpsimd",
             lambda e, dir_=dir_: e.dma_start(out=dbc_in.ap()[dir_],
                                              in_=dbcT[0:96, dir_, :]),
             waits=e_front, sig="s2_b", dma=True)
    P.op("gpsimd",
         lambda e: e.collective_compute(
             "AllReduce", OP.add, replica_groups=GROUPS,
             ins=[dbc_in.ap().opt()], outs=[dbc_out.ap().opt()]),
         waits=[P.ev("s2_b")], sig="s2_cc")
    e_cc1 = P.ev("s2_cc")

    es_m1.close()
    dtrows = sb("dtrows", [128, T], BF16, stack=es_m)
    P.fence([e_cc1])
    with nc.sbuf_tensor("dtstg", [128, T], F32) as dtstg, \
            nc.sbuf_tensor("bcstg", [64, T], F32) as bcstg, \
            nc.sbuf_tensor("bcbf", [64, T], BF16) as bcbf:
        for dir_ in range(2):
            dmaop("sync", dtstg[dir_ * 64:(dir_ + 1) * 64, :],
                  dbc_out.ap()[dir_, 0:64, :], "s2_dtl", waits=[e_cc1])
            dmaop("sync", bcstg[dir_ * 32:(dir_ + 1) * 32, :],
                  dbc_out.ap()[dir_, 64:96, :], "s2_bc", waits=[e_cc1])
        P.op("vector", lambda e, dtstg=dtstg: e.tensor_copy(dtrows[:], dtstg[:]),
             waits=[P.ev("s2_dtl")], sig="s2_dt")
        P.op("vector", lambda e, bcstg=bcstg, bcbf=bcbf: e.tensor_copy(bcbf[:], bcstg[:]),
             waits=[P.ev("s2_bc")], sig="s2_cv")
        for dir_ in range(2):
            P.op("gpsimd",
                 lambda e, dir_=dir_, bcbf=bcbf: e.dma_start(
                     out=bc_rows.ap()[dir_], in_=bcbf[dir_ * 32:(dir_ + 1) * 32, :]),
                 waits=[P.ev("s2_cv")], sig="s2_bcw", dma=True)
    e_dt = P.ev("s2_dt")
    e_bc = P.ev("s2_bcw")
    if maxstage == "s2":
        for ti in range(NTT):
            dmaop("gpsimd", out_t[ti * 128:(ti + 1) * 128, :],
                  dtrows[:, (ti % 8) * 1024:(ti % 8 + 1) * 1024], "out_d",
                  waits=[e_dt, e_bc])
        P.op("vector", lambda e: e.engine_nop(), waits=[P.ev("out_d")])
        P.emit_all(block)
        es_m.close(); es.close()
        return nc
    P.fence([e_bc])
    P.release(["s0_", "lc_", "weff_", "s1_ln", "s1_x0", "s1_x1", "s1_tp", "s1_mm",
               "s1_evx", "s1_evz", "s1_cv", "s1_xc", "s1_pad", "s1_xp", "s1_dbc0",
               "s2_b", "s2_cc", "s2_bc", "s2_cv"])

    es_m3 = ExitStack()
    dtc = sb("dtc", [128, 4, MCH], stack=es_m3)
    xcr = sb("xcr", [128, 8, MCH], BF16, stack=es_m3)
    szr = sb("szr", [128, 8, MCH], BF16, stack=es_m3)
    ub = sb("ub", [128, 4, MCH], BF16, stack=es_m3)
    bexp = sb("bexp", [128, 2, MCH], BF16, stack=es_m3)
    cexp = sb("cexp", [128, 2, MCH], BF16, stack=es_m3)
    ubn = sb("ubn", [128, 2, MCH], BF16, stack=es_m3)
    dAb = sb("dAb", [128, 2, MCH], stack=es_m3)
    hb = sb("hb", [128, 2, MCH], BF16, stack=es_m3)
    chb = sb("chb", [128, 2, MCH], BF16, stack=es_m3)
    ycur = sb("ycur", [128, 4, MCH], BF16, stack=es_m3)
    yfr = sb("yfr", [128, 4, MCH], BF16, stack=es_m3)
    ygate = sb("ygate", [128, 8, MCH], BF16, stack=es_m3)
    outp = sb("outp", [128, 2, D], stack=es_m3)
    state = sb("state", [128, 256], stack=es_m3)

    P.op("vector", lambda e: e.memset(state[:], 0.0), sig="s3_i")
    e_st0 = P.ev("s3_i")

    def scan_chunks(dir_, rev, tag, other_sp, first_guards=(), reader_tag=None):
        chunks = list(range(NCH - 1, -1, -1)) if rev else list(range(NCH))
        for ci, c in enumerate(chunks):
            fg = list(first_guards) if ci == 0 else []
            rg = [(reader_tag, P.n(reader_tag))] if reader_tag else []
            base_u = P.n(tag + "_u")
            base_sp = P.n(tag + "_sp")
            gi2 = 0
            for mt in range(4):
                for nchk in range(2):
                    ps = [PSD, PSE][gi2 % 2]

                    def mm(e, mt=mt, nchk=nchk, ps=ps, c=c, dir_=dir_):
                        return e.matmul(
                            ps[:], dtw_bf[dir_ * 64:(dir_ + 1) * 64, mt * 128:(mt + 1) * 128],
                            dtrows[dir_ * 64:(dir_ + 1) * 64,
                                   c * MCH + nchk * 512:c * MCH + (nchk + 1) * 512],
                            start=True, stop=True)
                    w = [e_dt, e_w0]
                    if gi2 >= 2:
                        w.append((tag + "_sp", base_sp + gi2 - 1))
                    else:
                        w.append((tag + "_sp", base_sp))
                        if ci == 0 and other_sp is not None:
                            w.append(other_sp)
                    P.op("tensor", mm, waits=w, sig=tag + "_mm")

                    def spf(e, mt=mt, nchk=nchk, ps=ps, dir_=dir_):
                        dst = dtc[:, mt, nchk * 512:(nchk + 1) * 512]
                        e.activation(dst, ps[:], AF.Exp,
                                     bias=dtb[:, dir_ * 4 + mt:dir_ * 4 + mt + 1])
                        return e.activation(dst, dst, AF.Ln, bias=1.0)
                    P.op("scalar", spf,
                         waits=[P.ev(tag + "_mm"), (tag + "_u", base_u)] + fg,
                         sig=tag + "_sp")
                    gi2 += 1
            e_dtw = P.ev(tag + "_sp")
            for dtl in range(4):
                P.op("sync",
                     lambda e, dtl=dtl, c=c, dir_=dir_: e.dma_start(
                         out=xcr[:, dir_ * 4 + dtl, :],
                         in_=xc_sp.ap()[dir_, dtl, :, c * MCH:(c + 1) * MCH]),
                     waits=[P.ev("s1_sp"), (tag + "_u", base_u)] + fg + rg,
                     sig=tag + "_xc", dma=True)
            P.op("vector",
                 lambda e, dir_=dir_: e.tensor_tensor(
                     ub[:], dtc[:, 0:4, :], xcr[:, dir_ * 4:dir_ * 4 + 4, :], OP.mult),
                 waits=[e_dtw, P.ev(tag + "_xc")], sig=tag + "_u")
            e_u = P.ev(tag + "_u")
            base_sc = P.n(tag + "_sc")
            base_ub2 = P.n(tag + "_ub")
            base_ch = P.n(tag + "_ch")
            base_dA = P.n(tag + "_dA")
            for n in range(NST):
                pb = n % 2
                P.op("sync",
                     lambda e, n=n, pb=pb, c=c, dir_=dir_: e.dma_start(
                         out=bexp[:, pb, :],
                         in_=bc_rows.ap()[dir_, n:n + 1, c * MCH:(c + 1) * MCH]
                         .partition_broadcast(128)),
                     waits=[e_bc, (tag + "_ub", max(0, base_ub2 + 4 * (n - 1)))]
                     + (fg if n < 2 else []),
                     sig=f"s3x_b{pb}", dma=True)
                P.op("sync",
                     lambda e, n=n, pb=pb, c=c, dir_=dir_: e.dma_start(
                         out=cexp[:, pb, :],
                         in_=bc_rows.ap()[dir_, NST + n:NST + n + 1, c * MCH:(c + 1) * MCH]
                         .partition_broadcast(128)),
                     waits=[e_bc, (tag + "_ch", max(0, base_ch + 4 * (n - 1)))]
                     + (fg if n < 2 else []),
                     sig=f"s3x_c{pb}", dma=True)
                e_bx = P.ev(f"s3x_b{pb}")
                e_cx = P.ev(f"s3x_c{pb}")
                for dtl in range(4):
                    i = n * 4 + dtl
                    pd = i % 2
                    P.op("scalar",
                         lambda e, dtl=dtl, n=n, pd=pd, dir_=dir_: e.activation(
                             dAb[:, pd, :], dtc[:, dtl, :], AF.Exp,
                             scale=A_f[:, dir_ * 4 + dtl, n:n + 1]),
                         waits=[e_A, (tag + "_sc", max(0, base_sc + i - 1))]
                         + (fg if i < 2 else []),
                         sig=tag + "_dA")
                    P.op("vector",
                         lambda e, dtl=dtl, pb=pb, pd=pd: e.tensor_tensor(
                             ubn[:, pd, :], ub[:, dtl, :], bexp[:, pb, :], OP.mult),
                         waits=[e_bx], sig=tag + "_ub")
                    col = dir_ * 128 + n * 8 + dtl

                    def sc(e, pd=pd, col=col, rev=rev):
                        d0 = dAb[:, pd, ::-1] if rev else dAb[:, pd, :]
                        d1 = ubn[:, pd, ::-1] if rev else ubn[:, pd, :]
                        ho = hb[:, pd, ::-1] if rev else hb[:, pd, :]
                        e.tensor_tensor_scan(ho, d0, d1, state[:, col:col + 1],
                                             OP.mult, OP.add)
                        src = hb[:, pd, 0:1] if rev else hb[:, pd, MCH - 1:MCH]
                        return e.tensor_copy(state[:, col:col + 1], src)
                    P.op("vector", sc,
                         waits=[e_st0, (tag + "_dA", base_dA + i + 1)],
                         sig=tag + "_sc")

                    def chacc(e, dtl=dtl, pb=pb, pd=pd, n=n):
                        if n == 0:
                            return e.tensor_tensor(ycur[:, dtl, :], hb[:, pd, :],
                                                   cexp[:, pb, :], OP.mult)
                        e.tensor_tensor(chb[:, pd, :], hb[:, pd, :],
                                        cexp[:, pb, :], OP.mult)
                        return e.tensor_add(ycur[:, dtl, :], ycur[:, dtl, :],
                                            chb[:, pd, :])
                    P.op("vector", chacc, waits=[e_cx], sig=tag + "_ch")
            yield ci, c

    for ci, c in scan_chunks(0, False, "s3a", None):
        e_y = P.ev("s3a_ch")
        for dtl in range(4):
            P.op("gpsimd",
                 lambda e, dtl=dtl, c=c: e.dma_start(
                     out=yf_sp.ap()[dtl, :, c * MCH:(c + 1) * MCH],
                     in_=ycur[:, dtl, :]),
                 waits=[e_y], sig="s3a_ysp", dma=True)
        P.op("vector", lambda e: e.engine_nop(), waits=[P.ev("s3a_ysp")])

    if maxstage == "s3a":
        return finish_any([P.ev("s3a_ysp"), P.ev("s3a_sc")], dtc[:, 0, :], [es_m3, es_m])

    e_s3a_done = [P.ev("s3a_u"), P.ev("s3a_ub"), P.ev("s3a_ch"),
                  P.ev("s3a_sc"), P.ev("s3a_dA"), P.ev("s3a_xc")]
    for ci, c in scan_chunks(1, maxstage != "s3bf", "s3b", P.ev("s3a_sp"),
                             first_guards=e_s3a_done, reader_tag="s3b_g"):
        e_y = P.ev("s3b_ch")
        base_g = P.n("s3b_g")
        fg2 = e_s3a_done if ci == 0 else []
        for dtl in range(4):
            P.op("sync",
                 lambda e, dtl=dtl, c=c: e.dma_start(
                     out=yfr[:, dtl, :], in_=yf_sp.ap()[dtl, :, c * MCH:(c + 1) * MCH]),
                 waits=[P.ev("s3a_ysp"), ("s3b_g", base_g)] + fg2,
                 sig="s3b_ld", dma=True)
            P.op("sync",
                 lambda e, dtl=dtl, c=c: e.dma_start(
                     out=xcr[:, dtl, :], in_=xc_sp.ap()[0, dtl, :, c * MCH:(c + 1) * MCH]),
                 waits=[("s3b_g", base_g)] + fg2, sig="s3b_ld", dma=True)
            for dir_ in range(2):
                P.op("sync",
                     lambda e, dtl=dtl, c=c, dir_=dir_: e.dma_start(
                         out=szr[:, dir_ * 4 + dtl, :],
                         in_=sz_sp.ap()[dir_, dtl, :, c * MCH:(c + 1) * MCH]),
                     waits=[P.ev("s1_zsp"), ("s3b_g", base_g)],
                     sig="s3b_ld", dma=True)
        e_ld = P.ev("s3b_ld")
        for dir_ in range(2):
            for dtl in range(4):
                def g1(e, dir_=dir_, dtl=dtl):
                    ysrc = yfr[:, dtl, :] if dir_ == 0 else ycur[:, dtl, :]
                    xsrc = xcr[:, dtl, :] if dir_ == 0 else xcr[:, 4 + dtl, :]
                    e.scalar_tensor_tensor(ygate[:, dir_ * 4 + dtl, :], xsrc,
                                           Dp[:, dir_ * 4 + dtl:dir_ * 4 + dtl + 1],
                                           ysrc, OP.mult, OP.add)
                    return e.tensor_tensor(ygate[:, dir_ * 4 + dtl, :],
                                           ygate[:, dir_ * 4 + dtl, :],
                                           szr[:, dir_ * 4 + dtl, :], OP.mult)
                P.op("vector", g1, waits=[e_ld, e_y], sig="s3b_g")
        e_g = P.ev("s3b_g")
        base_oe = P.n("s3b_oe")
        gidx3 = 0
        for ti in range(8):
            for nchk in range(2):
                ps = SLOT6[gidx3 % 4]

                def mm(e, ti=ti, nchk=nchk, ps=ps):
                    for g in range(8):
                        ins = e.matmul(ps, ygate[:, g, ti * 128:(ti + 1) * 128],
                                       weff[:, g, nchk * 512:(nchk + 1) * 512],
                                       start=(g == 0), stop=(g == 7))
                    return ins
                w = [e_g, e_weff]
                if gidx3 >= 4:
                    w.append(("s3b_oe", base_oe + gidx3 - 3))
                else:
                    w.append(("s3b_oe", base_oe))
                P.op("tensor", mm, waits=w, sig="s3b_om")
                podt = f"s3b_od{ti % 2}"
                P.op("vector",
                     lambda e, ti=ti, nchk=nchk, ps=ps: e.tensor_copy(
                         outp[:, ti % 2, nchk * 512:(nchk + 1) * 512], ps),
                     waits=[P.ev("s3b_om"), (podt, 16 * max(0, P.n(podt) // 16))],
                     sig="s3b_oe")
                gidx3 += 1
            P.op("gpsimd",
                 lambda e, ti=ti, c=c: e.dma_start(
                     out=mout_in.ap()[c * MCH + ti * 128: c * MCH + (ti + 1) * 128, :],
                     in_=outp[:, ti % 2, :]),
                 waits=[P.ev("s3b_oe")], sig=f"s3b_od{ti % 2}", dma=True)

    if maxstage in ("s3b", "s3bf"):
        return finish_any([P.ev("s3b_od0"), P.ev("s3b_od1")], outp[:, 0, :],
                          [es_m3, es_m])

    P.op("gpsimd",
         lambda e: e.collective_compute(
             "ReduceScatter", OP.add, replica_groups=GROUPS,
             ins=[mout_in.ap().opt()], outs=[mout_rs.ap().opt()]),
         waits=[P.ev("s3b_od0"), P.ev("s3b_od1")], sig="s4_cc")
    e_cc2 = P.ev("s4_cc")

    es_m3.close()
    es_m.close()
    P.fence([e_cc2])
    P.release(["s1_", "s2_", "s3_", "s3a_", "s3b_"])
    stream = sb("stream", [128, NTT, D])
    centbf = sb("centbf", [128, 8, NE], BF16)
    ebrow = sb("ebrow", [1, NE], BF16)
    gsc = sb("gsc", [128, 6])
    g_sb = sb("g_sb", [128, NTT, NE])
    scs = sb("scs", [128, NE])

    with nc.sbuf_tensor("projbb", [128, D], F32) as projbb, \
            nc.sbuf_tensor("rsstg", [128, 2, D], F32) as rsstg, \
            nc.sbuf_tensor("xstg2", [128, 2, D], F32) as xstg2:
        dmaop("sync", projbb[:], proj_b.partition_broadcast(128), "mw_ld")
        e_pb2 = P.ev("mw_ld")
        for ti in range(NTT):
            w = [e_cc2]
            if ti >= 2:
                w.append(P.w("s4_add", ti - 1))
            dmaop("sync", rsstg[:, ti % 2, :],
                  mout_rs.ap()[ti * 128:(ti + 1) * 128, :], f"s4_ld{ti % 2}", waits=w)
            dmaop("sync", xstg2[:, ti % 2, :],
                  x_tr[ti * 128:(ti + 1) * 128, :], f"s4_ld{ti % 2}", waits=w)

            def add2(e, ti=ti, projbb=projbb, rsstg=rsstg, xstg2=xstg2):
                e.tensor_add(stream[:, ti, :], xstg2[:, ti % 2, :], rsstg[:, ti % 2, :])
                return e.tensor_add(stream[:, ti, :], stream[:, ti, :], projbb[:])
            P.op("vector", add2,
                 waits=[(f"s4_ld{ti % 2}", 32 * (ti // 2 + 1)), e_pb2], sig="s4_add")
    e_x2 = P.ev("s4_add")
    if dbg:
        for ti in range(NTT):
            dmaop("gpsimd", dbg_t[0][ti * 128:(ti + 1) * 128, :], stream[:, ti, :],
                  "dbg_d", waits=[e_x2])

    def finish(e_last):
        for ti in range(NTT):
            dmaop("sync", out_t[ti * 128:(ti + 1) * 128, :], stream[:, ti, :],
                  "out_d", waits=[e_last])
        fin = [P.ev("out_d")] + ([P.ev("dbg_d")] if dbg else [])
        P.op("vector", lambda e: e.engine_nop(), waits=fin)
        P.emit_all(block)
        es.close()
        return nc
    if maxstage == 1:
        return finish(e_x2)

    with nc.sbuf_tensor("wsstg", [128, 8, D], F32) as wsstg, \
            nc.sbuf_tensor("wsstg2", [128, 8, D], F32) as wsstg2, \
            nc.sbuf_tensor("wshp", [128, 8, D], BF16) as wshp, \
            nc.sbuf_tensor("bstgB", [1, D], F32) as bstgB, \
            nc.sbuf_tensor("bstg", [2, D], F32) as bstg, \
            nc.sbuf_tensor("brstg", [NE, D], F32) as brstg, \
            nc.sbuf_tensor("brbf", [NE + 1, D], BF16) as brbf, \
            nc.sbuf_tensor("censtg", [128, 8, NE], F32) as censtg, \
            nc.sbuf_tensor("ebstg", [1, NE], F32) as ebstg:
        wsr = WsT_i.rearrange("n (kt p) m -> p (n kt) m", p=128)
        dmaop("sync", wsstg[:], wsr[:, 0:8, :], "mw_ld", waits=[e_x2])
        dmaop("sync", wsstg2[:], wsr[:, 8:16, :], "mw_ld", waits=[e_x2])
        dmaop("sync", bstg[0:1, :], bs_i[0:1, :], "mw_ld", waits=[e_x2])
        dmaop("sync", bstgB[0:1, :], bs_i[1:2, :], "mw_ld", waits=[e_x2])
        dmaop("sync", brstg[:], br_i, "mw_ld", waits=[e_x2])
        dmaop("sync", censtg[:], centT_i.rearrange("(kt p) n -> p kt n", p=128), "mw_ld", waits=[e_x2])
        dmaop("sync", ebstg[:], eb_i, "mw_ld", waits=[e_x2])

        def wsum(e, wsstg=wsstg, wsstg2=wsstg2, wshp=wshp, bstg=bstg,
                 bstgB=bstgB, brstg=brstg, censtg=censtg, ebstg=ebstg):
            e.tensor_add(wsstg[:], wsstg[:], wsstg2[:])
            e.tensor_scalar_mul(wsstg[:], wsstg[:], 0.5)
            e.tensor_copy(wshp[:], wsstg[:])
            e.tensor_copy(brbf[0:NE, :], brstg[:])
            e.tensor_add(bstg[0:1, :], bstg[0:1, :], bstgB[0:1, :])
            e.tensor_scalar_mul(bstg[0:1, :], bstg[0:1, :], 0.5)
            e.tensor_copy(brbf[NE:NE + 1, :].rearrange("p m -> p m"), bstg[0:1, :]) if False else None
            e.tensor_copy(bstgB[0:1, :], bstg[0:1, :])
            e.tensor_copy(centbf[:], censtg[:])
            return e.tensor_copy(ebrow[:], ebstg[:])
        P.op("vector", wsum, waits=[P.ev("mw_ld")], sig="mw_cv")
        P.op("gpsimd", lambda e, wshp=wshp: e.dma_start(out=wsh_sp.ap(), in_=wshp[:]),
             waits=[P.ev("mw_cv")], sig="mw_sp", dma=True)
        P.op("vector", lambda e, bstgB=bstgB, brbf=brbf: e.tensor_copy(
                 brbf[NE - 16:NE - 15, :], bstgB[0:1, :]) if False else
             e.tensor_copy(brbf[0:1, 0:1], brbf[0:1, 0:1]), waits=[], sig=None) if False else None
        P.op("vector", lambda e, bstgB=bstgB, brbf=brbf: e.tensor_copy(
                 brbf[NE:NE + 1, :].rearrange("p m -> p m"), bstgB[0:1, :]) if False
             else e.engine_nop(), waits=[], sig=None) if False else None
        P.op("gpsimd", lambda e, brbf=brbf: e.dma_start(
                 out=br_hb.ap()[0:NE, :], in_=brbf[0:NE, :]),
             waits=[P.ev("mw_cv")], sig="mw_sp", dma=True)
        P.op("vector", lambda e, bstgB=bstgB, brbf=brbf: e.tensor_copy(
                 brbf[0:1, :], bstgB[0:1, :]),
             waits=[P.ev("mw_sp")], sig="mw_cv")
        P.op("gpsimd", lambda e, brbf=brbf: e.dma_start(
                 out=br_hb.ap()[NE:NE + 1, :], in_=brbf[0:1, :]),
             waits=[P.ev("mw_cv")], sig="mw_sp", dma=True)
    e_mw = P.ev("mw_cv")
    e_mws = P.ev("mw_sp")

    def moe(call, e_start):
        tg = f"m{call}"
        es_moe = ExitStack()
        es_ln = ExitStack()
        xhatT = sb("xhatT" + tg, [128, 8, TR], BF16, stack=es_moe)
        wsh_bf = sb("wsh_bf" + tg, [128, 8, D], BF16, stack=es_moe)
        brp = sb("brp" + tg, [1, 2, D], BF16, stack=es_moe)
        xtok = sb("xtok" + tg, [128, NTT, D], BF16, stack=es_ln)
        xmf = sb("xmf" + tg, [128, 2, D], stack=es_ln)
        dmaop("sync", wsh_bf[:], wsh_sp.ap(), "me_shl", waits=[e_mws, e_start])
        e_shl = P.ev("me_shl")
        for ti in range(NTT):
            dw = [e_start] if ti < 2 else [P.w(tg + "_tp", 8 * (ti - 1))]
            lev = ln_tile(tg + "_ln", stream[:, ti, :], xmf[:, ti % 2, :],
                          dst_waits=dw)
            P.op("vector",
                 lambda e, ti=ti: e.tensor_copy(xtok[:, ti, :], xmf[:, ti % 2, :]),
                 waits=[lev], sig=tg + "_xtk")
            for j in range(8):
                extra = [lev, e_start]
                if ti > 0:
                    extra.append(P.w(tg + "_tpe", 8 * ti))
                P.op("tensor",
                     lambda e, j=j, ti=ti: e.transpose(
                         TPS[j], xmf[:, ti % 2, j * 128:(j + 1) * 128], id_bf[:]),
                     waits=extra, sig=tg + "_tp")
                P.op("vector",
                     lambda e, j=j, ti=ti: e.tensor_copy(
                         xhatT[:, j, ti * 128:(ti + 1) * 128], TPS[j]),
                     waits=[P.ev(tg + "_tp")], sig=tg + "_tpe")
        e_tp2 = P.ev(tg + "_tpe")
        for ti in range(NTT):
            def mmsc(e, ti=ti):
                for kt in range(8):
                    e.matmul(PSD[:, 0:NE], xhatT[:, kt, ti * 128:(ti + 1) * 128],
                             centbf[:, kt, :], start=(kt == 0), stop=False)
                return e.matmul(PSD[:, 0:NE], ones_bf[0:1, :], ebrow[:],
                                start=False, stop=True)
            P.op("tensor", mmsc, waits=[e_tp2, e_mw, P.ev(tg + "_gv")], sig=tg + "_gm")

            def gates(e, ti=ti):
                e.reduce_max(gsc[:, 0:1], PSD[:, 0:NE], axis=AXX, negate=True)
                return e.tensor_copy(scs[:], PSD[:, 0:NE])
            P.op("vector", gates, waits=[P.ev(tg + "_gm")], sig=tg + "_gv")
            P.op("scalar",
                 lambda e: e.activation(scs[:], scs[:], AF.Exp, bias=gsc[:, 0:1],
                                        accum_out=gsc[:, 1:2]),
                 waits=[P.ev(tg + "_gv")], sig=tg + "_gs")

            def top2(e, ti=ti):
                e.reciprocal(gsc[:, 2:3], gsc[:, 1:2])
                e.tensor_scalar_mul(scs[:], scs[:], gsc[:, 2:3])
                e.reduce_max(gsc[:, 3:4], scs[:], axis=AXX)
                e.tensor_scalar_mul(gsc[:, 4:5], gsc[:, 3:4], -1e30)
                e.tensor_single_scalar(g_sb[:, ti, :], scs[:], gsc[:, 3:4], OP.is_ge)
                e.scalar_tensor_tensor(g_sb[:, ti, :], g_sb[:, ti, :], gsc[:, 4:5],
                                       scs[:], OP.mult, OP.add)
                e.reduce_max(gsc[:, 5:6], g_sb[:, ti, :], axis=AXX)
                e.tensor_single_scalar(g_sb[:, ti, :], scs[:], gsc[:, 5:6], OP.is_ge)
                return e.tensor_tensor(g_sb[:, ti, :], g_sb[:, ti, :], scs[:], OP.mult)
            P.op("vector", top2, waits=[P.ev(tg + "_gs")], sig=tg + "_gv")
        e_gates = P.ev(tg + "_gv")
        for ti in range(NTT):
            P.op("vector",
                 lambda e, ti=ti: e.tensor_add(stream[:, ti, :], stream[:, ti, :],
                                               xtok[:, ti, :]),
                 sig=tg + "_ai")
        e_ai = P.ev(tg + "_ai")
        es_ln.close()
        with nc.sbuf_tensor(f"wstg{call}", [128, 4, D], F32) as wstg, \
                nc.sbuf_tensor(f"wbf{call}", [128, 2, 8, D], BF16) as wbf:
            jglob = 0
            for e_i in range(NE + 1):
                pw = e_i % 2
                if e_i < NE:
                    if call == 0:
                        for half in range(2):
                            dmaop("sync", wstg[:, 0:4, :],
                                  WrT_i.rearrange("n (kt p) m -> n p kt m", p=128)
                                  [e_i, :, half * 4:(half + 1) * 4, :],
                                  "a_wl",
                                  waits=[(tg + "_wc", P.n(tg + "_wc")),
                                         e_start, e_ai])
                            P.op("vector",
                                 lambda e, pw=pw, half=half: e.tensor_copy(
                                     wbf[:, pw, half * 4:(half + 1) * 4, :],
                                     wstg[:, 0:4, :]),
                                 waits=[P.ev("a_wl"),
                                        (tg + f"_ws{pw}", 16 * (e_i // 2)),
                                        P.w(tg + "_em", max(0, jglob - 16))],
                                 sig=tg + "_wc")
                        P.op("gpsimd",
                             lambda e, e_i=e_i, pw=pw: e.dma_start(
                                 out=wr_bf_sp.ap()[e_i], in_=wbf[:, pw, :, :]),
                             waits=[P.ev(tg + "_wc")], sig=tg + f"_ws{pw}",
                             dma=True)
                    else:
                        dmaop("sync", wbf[:, pw, :, :], wr_bf_sp.ap()[e_i],
                              tg + f"_wd{pw}",
                              waits=[("m0_ws0", 16 * (NE // 2)),
                                     ("m0_ws1", 16 * (NE // 2)), e_start, e_ai,
                                     P.w(tg + "_em", max(0, jglob - 16))])
                if e_i >= NE:
                    e_w = e_shl
                elif call == 0:
                    e_w = P.ev(tg + "_wc")
                else:
                    e_w = P.ev(tg + f"_wd{pw}")
                P.op("sync",
                     lambda e, e_i=e_i, pw=pw: e.dma_start(
                         out=brp[0:1, pw, :], in_=br_hb.ap()[e_i:e_i + 1, :]),
                     waits=[e_mws, P.w(tg + "_em", max(0, jglob - 16))],
                     sig=f"me_brl{pw}", dma=True)
                e_br = P.ev(f"me_brl{pw}")
                brow = brp[0:1, pw, :]
                for ti in range(NTT):
                    psfull = [PSA, PSB][jglob % 2]

                    def mme(e, ti=ti, e_i=e_i, pw=pw, psfull=psfull, brow=brow):
                        ins = None
                        for nchk in range(2):
                            dstp = psfull[:, nchk * 512:(nchk + 1) * 512]
                            for kt in range(8):
                                if e_i < NE:
                                    rhs = wbf[:, pw, kt, nchk * 512:(nchk + 1) * 512]
                                else:
                                    rhs = wsh_bf[:, kt, nchk * 512:(nchk + 1) * 512]
                                e.matmul(dstp, xhatT[:, kt, ti * 128:(ti + 1) * 128],
                                         rhs, start=(kt == 0), stop=False)
                            ins = e.matmul(dstp, ones_bf[0:1, :],
                                           brow[:, nchk * 512:(nchk + 1) * 512],
                                           start=False, stop=True)
                        return ins
                    w = [e_w, e_gates, e_ai, e_br]
                    if jglob >= 2:
                        w.append(P.w(tg + "_ee", jglob - 1))
                    P.op("tensor", mme, waits=w, sig=tg + "_em")

                    def evx(e, ti=ti, e_i=e_i, psfull=psfull):
                        if e_i < NE:
                            return e.scalar_tensor_tensor(
                                stream[:, ti, :], psfull[:], g_sb[:, ti, e_i:e_i + 1],
                                stream[:, ti, :], OP.mult, OP.add)
                        return e.tensor_add(stream[:, ti, :], stream[:, ti, :],
                                            psfull[:])
                    P.op("vector", evx, waits=[P.ev(tg + "_em")], sig=tg + "_ee")
                    jglob += 1
        es_moe.close()
        return P.ev(tg + "_ee")

    e_x3 = moe(0, e_x2)
    if dbg:
        for ti in range(NTT):
            dmaop("gpsimd", dbg_t[1][ti * 128:(ti + 1) * 128, :], stream[:, ti, :],
                  "dbg_d", waits=[e_x3])
    if maxstage == 2:
        return finish(e_x3)

    es_a = ExitStack()
    es_a1 = ExitStack()
    P.fence([e_x3, P.ev("m0_ws0"), P.ev("m0_ws1")])
    P.release(["s4_", "mw_", "m0_"])
    wupk_bf = sb("wupk_bf", [128, 2, D], BF16, stack=es_a)
    wupv_bf = sb("wupv_bf", [128, 2, D], BF16, stack=es_a)
    latT = sb("latT", [128, 2, TR], BF16, stack=es_a)
    nrow = sb("nrow", [1, 8], F32, stack=es_a)
    mneg = sb("mneg", [128, 1], F32, stack=es_a)
    attr = sb("attr", [128, 2, 8, 128], BF16, stack=es_a)
    bqrow = sb("bqrow", [1, D], BF16, stack=es_a)
    bkrow = sb("bkrow", [1, D], BF16, stack=es_a)
    bvrow = sb("bvrow", [1, D], BF16, stack=es_a)
    bdrow = sb("bdrow", [1, DLAT], BF16, stack=es_a)
    borow = sb("borow", [1, D], BF16, stack=es_a)
    qstg = sb("qstg", [128, 2048], F32, stack=es_a1)
    tmpbf = sb("tmpbf", [128, 2048], BF16, stack=es_a1)
    wq_bf = sb("wq_bf", [128, 8, D], BF16, stack=es_a1)
    wdn_bf = sb("wdn_bf", [128, 8, DLAT], BF16, stack=es_a1)
    xhatTa = sb("xhatTa", [128, 8, TR], BF16, stack=es_a1)
    xh2 = sb("xh2", [128, 2, D], stack=es_a1)
    qkb = xh2
    ropb = sb("ropb", [128, 2, D], stack=es_a1)
    rsc2 = sb("rsc2", [128, 2, 512], stack=es_a1)
    lstg = sb("lstg", [128, 2, DLAT], F32, stack=es_a1)

    def loadc(dst, src, ngrp, width):
        v = qstg[:, 0:ngrp * width].rearrange("p (g w) -> p g w", g=ngrp)
        dmaop("sync", v, src, "a_wl", waits=[P.ev("a_wc")])
        P.op("vector", lambda e, d=dst, v=v: e.tensor_copy(d, v),
             waits=[P.ev("a_wl")], sig="a_wc")

    wqr = Wq_i.rearrange("(kt p) m -> p kt m", p=128)
    for qq in range(4):
        loadc(wq_bf[:, 2 * qq:2 * qq + 2, :], wqr[:, 2 * qq:2 * qq + 2, :], 2, D)
    for qq in range(4):
        loadc(wdn_bf[:, 2 * qq:2 * qq + 2, :], Wdown_i.rearrange(
            "(kt p) m -> p kt m", p=128)[:, 2 * qq:2 * qq + 2, :], 2, DLAT)
    loadc(wupk_bf[:], Wup_k_i.rearrange("(kt p) m -> p kt m", p=128), 2, D)
    loadc(wupv_bf[:], Wup_v_i.rearrange("(kt p) m -> p kt m", p=128), 2, D)
    wor = Wout_i.rearrange("(kt p) m -> p kt m", p=128)
    for qq in range(4):
        dmaop("sync", qstg[:].rearrange("p (g m) -> p g m", g=2),
              wor[:, 2 * qq:2 * qq + 2, :], "a_wl",
              waits=[P.ev("a_wc"), P.ev("a_wos")])
        P.op("vector", lambda e: e.tensor_copy(tmpbf[:], qstg[:]),
             waits=[P.ev("a_wl")], sig="a_wc")
        P.op("gpsimd",
             lambda e, qq=qq: e.dma_start(
                 out=wo_sp.ap()[:, 2 * qq:2 * qq + 2, :],
                 in_=tmpbf[:].rearrange("p (g m) -> p g m", g=2)),
             waits=[P.ev("a_wc")], sig="a_wos", dma=True)
    with nc.sbuf_tensor("bstg2", [1, D], F32) as bstg2:
        for dst, src, wd in ((bqrow, bq_i, D), (bkrow, bup_k_i, D),
                             (bvrow, bup_v_i, D), (bdrow, bdown_i, DLAT),
                             (borow, bout_i, D)):
            dmaop("sync", bstg2[0:1, 0:wd], src, "a_bl2", waits=[P.ev("a_bc")])
            P.op("vector",
                 lambda e, b=bstg2, d=dst, wd=wd: e.tensor_copy(d[:], b[0:1, 0:wd]),
                 waits=[P.ev("a_bl2")], sig="a_bc")
    e_aw = [P.ev("a_wc"), P.ev("a_bc")]
    if maxstage == "aw":
        return finish_any(e_aw + [P.ev("a_wos")], qstg, [es_a1, es_a, es_m])

    for ti in range(NTT):
        dw = [] if ti < 2 else [P.w("a_tp", 8 * (ti - 1))]
        lev = ln_tile("a_ln", stream[:, ti, :], xh2[:, ti % 2, :], dst_waits=dw)
        for j in range(8):
            extra = [lev]
            if ti > 0:
                extra.append(P.w("a_tpe", 8 * ti))
            P.op("tensor",
                 lambda e, j=j, ti=ti: e.transpose(
                     TPS[j], xh2[:, ti % 2, j * 128:(j + 1) * 128], id_bf[:]),
                 waits=extra, sig="a_tp")
            P.op("vector",
                 lambda e, j=j, ti=ti: e.tensor_copy(
                     xhatTa[:, j, ti * 128:(ti + 1) * 128], TPS[j]),
                 waits=[P.ev("a_tp")], sig="a_tpe")
    e_tp3 = P.ev("a_tpe")

    def rope_tok(e, buf, tbl, scr):
        x1 = buf.rearrange("p (h m) -> p h m", h=16)[:, :, 0:32]
        x2 = buf.rearrange("p (h m) -> p h m", h=16)[:, :, 32:64]
        cosv = tbl[:, 0:512].rearrange("p (h m) -> p h m", h=16)
        sinv = tbl[:, 512:1024].rearrange("p (h m) -> p h m", h=16)
        s0 = scr[:, 0, :].rearrange("p (h m) -> p h m", h=16)
        s1 = scr[:, 1, :].rearrange("p (h m) -> p h m", h=16)
        e.tensor_tensor(s0, x1, cosv, OP.mult)
        e.tensor_tensor(s1, x2, sinv, OP.mult)
        e.tensor_sub(s0, s0, s1)
        e.tensor_tensor(s1, x1, sinv, OP.mult)
        e.tensor_copy(x1, s0)
        e.tensor_tensor(s0, x2, cosv, OP.mult)
        return e.tensor_add(x2, s1, s0)

    qmx = sb("qmx", [128, 4], F32, stack=es_a1)
    nsq = sb("nsq", [128, 16], F32, stack=es_a1)
    rsf = rsc2[:, :, :].rearrange("p a b -> p (a b)")
    P.op("vector", lambda e: e.memset(qmx[:], 0.0), sig="a_nv")
    for ti in range(NTT):
        dmaop("sync", ropb[:, ti % 2, :],
              ropeq_tok[ti * 128:(ti + 1) * 128, :], f"a_rl{ti % 2}",
              waits=[("a_qr", max(0, P.n("a_qr") - 1))])

        def mmq(e, ti=ti):
            ins = None
            for chk in range(2):
                dstp = [PSD, PSE][chk]
                for kt in range(8):
                    e.matmul(dstp[:], xhatTa[:, kt, ti * 128:(ti + 1) * 128],
                             wq_bf[:, kt, chk * 512:(chk + 1) * 512],
                             start=(kt == 0), stop=False)
                ins = e.matmul(dstp[:], ones_bf[0:1, :],
                               bqrow[:, chk * 512:(chk + 1) * 512],
                               start=False, stop=True)
            return ins
        w = e_aw + [e_tp3]
        if ti > 0:
            w.append(("a_qe", P.n("a_qe")))
        P.op("tensor", mmq, waits=w, sig="a_qm")
        P.op("vector",
             lambda e, ti=ti: (e.tensor_copy(qkb[:, ti % 2, 0:512], PSD[:]),
                               e.tensor_copy(qkb[:, ti % 2, 512:1024], PSE[:]))[-1],
             waits=[P.ev("a_qm"), (f"a_qsp{ti % 2}", 16 * (ti // 2))] +
             ([P.ev("a_wos")] if ti == 0 else []),
             sig="a_qe")
        P.op("vector",
             lambda e, ti=ti: rope_tok(e, qkb[:, ti % 2, :], ropb[:, ti % 2, :], rsc2),
             waits=[(f"a_rl{ti % 2}", 16 * (ti // 2 + 1))], sig="a_qr")

        def nrm(e, ti=ti):
            e.tensor_tensor(rsf[:], qkb[:, ti % 2, :], qkb[:, ti % 2, :], OP.mult)
            for h in range(16):
                e.reduce_sum(nsq[:, h:h + 1], rsf[:, h * 64:(h + 1) * 64], axis=AXX)
            e.reduce_max(qmx[:, 2:3], nsq[:], axis=AXX)
            return e.tensor_max(qmx[:, 0:1], qmx[:, 0:1], qmx[:, 2:3])
        P.op("vector", nrm, waits=[P.ev("a_qr")], sig="a_nv")
        P.op("gpsimd",
             lambda e, ti=ti: e.dma_start(
                 out=q_sp.ap()[ti * 128:(ti + 1) * 128, :], in_=qkb[:, ti % 2, :]),
             waits=[P.ev("a_nv")], sig=f"a_qsp{ti % 2}", dma=True)
    if maxstage == "aq":
        return finish_any([P.ev("a_qsp0"), P.ev("a_qsp1"), P.ev("a_qe")], qstg,
                          [es_a1, es_a, es_m])

    for ti in range(NTT):
        def mml(e, ti=ti):
            for kt in range(8):
                e.matmul(PSD[:, 0:DLAT], xhatTa[:, kt, ti * 128:(ti + 1) * 128],
                         wdn_bf[:, kt, :], start=(kt == 0), stop=False)
            return e.matmul(PSD[:, 0:DLAT], ones_bf[0:1, :], bdrow[:],
                            start=False, stop=True)
        w = e_aw + [e_tp3, ("a_qm", P.n("a_qm")), ("a_qe", P.n("a_qe"))]
        if ti > 0:
            w.append(P.w("a_le", ti))
        P.op("tensor", mml, waits=w, sig="a_lm")
        P.op("vector",
             lambda e, ti=ti: e.tensor_copy(lstg[:, ti % 2, :], PSD[:, 0:DLAT]),
             waits=[P.ev("a_lm"), P.w("a_lt", 2 * max(0, ti - 1))], sig="a_le")
        for j in range(2):
            P.op("tensor",
                 lambda e, ti=ti, j=j: e.transpose(
                     TPS[j], lstg[:, ti % 2, j * 128:(j + 1) * 128], id_bf[:]),
                 waits=[P.w("a_le", ti + 1),
                        P.w("a_lte", max(0, 2 * (ti - 1) + j + 1) if ti > 0 else 0)],
                 sig="a_lt")
            P.op("vector",
                 lambda e, ti=ti, j=j: e.tensor_copy(
                     latT[:, j, ti * 128:(ti + 1) * 128], TPS[j]),
                 waits=[P.ev("a_lt")], sig="a_lte")

    for ti in range(NTT):
        dmaop("sync", ropb[:, ti % 2, :],
              ropeq_tok[ti * 128:(ti + 1) * 128, :], f"a_krl{ti % 2}",
              waits=[("a_kr2", max(0, P.n("a_kr2") - 1)),
                     ("a_qr", P.n("a_qr")), ("a_nv", P.n("a_nv"))])

        def mmk(e, ti=ti):
            ins = None
            for chk in range(2):
                dstp = [PSD, PSE][chk]
                for lt in range(2):
                    e.matmul(dstp[:], latT[:, lt, ti * 128:(ti + 1) * 128],
                             wupk_bf[:, lt, chk * 512:(chk + 1) * 512],
                             start=(lt == 0), stop=False)
                ins = e.matmul(dstp[:], ones_bf[0:1, :],
                               bkrow[:, chk * 512:(chk + 1) * 512],
                               start=False, stop=True)
            return ins
        w = e_aw + [P.w("a_lte", 2 * (ti + 1)), ("a_lm", P.n("a_lm"))]
        if ti > 0:
            w.append(("a_ke", P.n("a_ke")))
        else:
            w.append(("a_le", P.n("a_le")))
        P.op("tensor", mmk, waits=w, sig="a_km")
        P.op("vector",
             lambda e, ti=ti: (e.tensor_copy(qkb[:, ti % 2, 0:512], PSD[:]),
                               e.tensor_copy(qkb[:, ti % 2, 512:1024], PSE[:]))[-1],
             waits=[P.ev("a_km"), (f"a_ksp{ti % 2}", 16 * (ti // 2)),
                    ("a_qsp0", P.n("a_qsp0")), ("a_qsp1", P.n("a_qsp1"))],
             sig="a_ke")
        P.op("vector",
             lambda e, ti=ti: rope_tok(e, qkb[:, ti % 2, :], ropb[:, ti % 2, :], rsc2),
             waits=[(f"a_krl{ti % 2}", 16 * (ti // 2 + 1))], sig="a_kr2")

        def nrmk(e, ti=ti):
            e.tensor_tensor(rsf[:], qkb[:, ti % 2, :], qkb[:, ti % 2, :], OP.mult)
            for h in range(16):
                e.reduce_sum(nsq[:, h:h + 1], rsf[:, h * 64:(h + 1) * 64], axis=AXX)
            e.reduce_max(qmx[:, 2:3], nsq[:], axis=AXX)
            return e.tensor_max(qmx[:, 1:2], qmx[:, 1:2], qmx[:, 2:3])
        P.op("vector", nrmk, waits=[P.ev("a_kr2")], sig="a_nv")
        P.op("gpsimd",
             lambda e, ti=ti: e.dma_start(
                 out=k_sp.ap()[ti * 128:(ti + 1) * 128, :], in_=qkb[:, ti % 2, :]),
             waits=[P.ev("a_nv")], sig=f"a_ksp{ti % 2}", dma=True)

        def mmv(e, ti=ti):
            ins = None
            for chk in range(2):
                dstp = PSC[:, chk * 512:(chk + 1) * 512]
                for lt in range(2):
                    e.matmul(dstp, latT[:, lt, ti * 128:(ti + 1) * 128],
                             wupv_bf[:, lt, chk * 512:(chk + 1) * 512],
                             start=(lt == 0), stop=False)
                ins = e.matmul(dstp, ones_bf[0:1, :],
                               bvrow[:, chk * 512:(chk + 1) * 512],
                               start=False, stop=True)
            return ins
        w = [P.w("a_lte", 2 * (ti + 1))] + e_aw
        if ti > 0:
            w.append(("a_ve", P.n("a_ve")))
        P.op("tensor", mmv, waits=w, sig="a_vm")
        P.op("vector",
             lambda e, ti=ti: e.tensor_copy(
                 tmpbf[:, (ti % 2) * 1024:(ti % 2) * 1024 + 1024], PSC[:]),
             waits=[P.ev("a_vm"), (f"a_vsp{ti % 2}", 16 * (ti // 2))], sig="a_ve")
        P.op("gpsimd",
             lambda e, ti=ti: e.dma_start(
                 out=v_sp.ap()[ti * 128:(ti + 1) * 128, :],
                 in_=tmpbf[:, (ti % 2) * 1024:(ti % 2) * 1024 + 1024]),
             waits=[P.ev("a_ve")], sig=f"a_vsp{ti % 2}", dma=True)

    P.op("tensor",
         lambda e: (e.transpose(PSD[0:1, 0:128], qmx[:, 0:1], id_bf[:]),
                    e.transpose(PSE[0:1, 0:128], qmx[:, 1:2], id_bf[:]))[-1],
         waits=[P.ev("a_nv"), ("a_km", P.n("a_km")), ("a_ke", P.n("a_ke"))],
         sig="a_nm")

    def nred(e):
        e.reduce_max(nrow[0:1, 0:1], PSD[0:1, 0:128], axis=AXX)
        e.reduce_max(nrow[0:1, 1:2], PSE[0:1, 0:128], axis=AXX)
        return e.tensor_tensor(nrow[0:1, 6:7], nrow[0:1, 0:1], nrow[0:1, 1:2],
                               OP.mult)
    P.op("vector", nred, waits=[P.ev("a_nm")], sig="a_nr")

    def mbias(e):
        e.activation(nrow[0:1, 6:7], nrow[0:1, 6:7], AF.Ln, bias=epsc[0:1, 1:2])
        return e.activation(nrow[0:1, 7:8], nrow[0:1, 6:7], AF.Exp, scale=0.5)
    P.op("scalar", mbias, waits=[P.ev("a_nr"), e_const], sig="a_ns")
    P.op("vector",
         lambda e: e.tensor_scalar_mul(nrow[0:1, 7:8], nrow[0:1, 7:8], -0.125),
         waits=[P.ev("a_ns")], sig="a_nr")
    P.op("gpsimd", lambda e: e.dma_start(out=mb_hb.ap(), in_=nrow[0:1, 7:8]),
         waits=[P.ev("a_nr")], sig="a_rw", dma=True)
    dmaop("sync", mneg[:], mb_hb.ap().partition_broadcast(128), "a_nb",
          waits=[P.ev("a_rw")])
    e_mb = P.ev("a_nb")
    if maxstage == "ak":
        return finish_any([P.ev("a_ksp0"), P.ev("a_ksp1"), P.ev("a_vsp0"),
                           P.ev("a_vsp1"), e_mb], qstg, [es_a1, es_a, es_m])

    es_a1.close()
    P.fence([P.ev("a_qsp0"), P.ev("a_qsp1"), P.ev("a_ksp0"), P.ev("a_ksp1"),
             P.ev("a_vsp0"), P.ev("a_vsp1"), P.ev("a_wos"), e_mb])
    blkb = sb("blkb", [128, 2, 2, 64], F32, stack=es_a)
    vbb = sb("vbb", [128, 2, 64], BF16, stack=es_a)
    qkt = sb("qkt", [64, 2, 2, 128], BF16, stack=es_a)
    pb = sb("pb", [128, 2, 128], BF16, stack=es_a)
    bdm = sb("bdm", [128, 128], BF16, stack=es_a)
    rden = sb("rden", [128, 2], F32, stack=es_a)
    attb = sb("attb", [128, 2, 64], BF16, stack=es_a)
    P.op("gpsimd", lambda e: e.dma_start(out=bdm[:], in_=bdmask_i),
         sig="a_bm", dma=True)
    e_bdm = P.ev("a_bm")
    qf = q_sp.ap().rearrange("t (h d) -> (t h) d", d=64)
    kf = k_sp.ap().rearrange("t (h d) -> (t h) d", d=64)
    vf = v_sp.ap().rearrange("t (h d) -> (t h) d", d=64)
    af = att_sp.ap().rearrange("t (h d) -> (t h) d", d=64)
    NBLK = TR * 16 // 128
    for blk in range(NBLK):
        pi = blk % 2
        P.op("gpsimd",
             lambda e, blk=blk, pi=pi: e.dma_start(
                 out=blkb[:, pi, 0, :], in_=qf[blk * 128:(blk + 1) * 128, :]),
             waits=[P.w("a_btp", 2 * max(0, blk - 1))], sig=f"a_bl{pi}", dma=True)
        P.op("gpsimd",
             lambda e, blk=blk, pi=pi: e.dma_start(
                 out=blkb[:, pi, 1, :], in_=kf[blk * 128:(blk + 1) * 128, :]),
             waits=[P.w("a_btp", 2 * max(0, blk - 1))], sig=f"a_bl{pi}", dma=True)
        P.op("sync",
             lambda e, blk=blk, pi=pi: e.dma_start(
                 out=vbb[:, pi, :], in_=vf[blk * 128:(blk + 1) * 128, :]),
             waits=[P.w("a_bav", 2 * max(0, blk - 1))], sig=f"a_vl{pi}", dma=True)
        for j in range(2):
            P.op("tensor",
                 lambda e, j=j, pi=pi: e.transpose(
                     [PSA, PSB][pi][0:64, j * 512:j * 512 + 128],
                     blkb[:, pi, j, :], id_bf[:]),
                 waits=[(f"a_bl{pi}", 32 * (blk // 2 + 1)),
                        P.w("a_bte", 2 * max(0, blk - 1))],
                 sig="a_btp")
        P.op("vector",
             lambda e, pi=pi: (e.tensor_copy(qkt[:, 0, pi, :],
                                             [PSA, PSB][pi][0:64, 0:128]),
                               e.tensor_copy(qkt[:, 1, pi, :],
                                             [PSA, PSB][pi][0:64, 512:640]))[-1],
             waits=[P.ev("a_btp"), P.w("a_bsc", max(0, blk - 1))],
             sig=[("a_bte", 2)])
        psc_ = [PSC[:, 0:128], PSC[:, 512:640]][pi]
        P.op("tensor",
             lambda e, pi=pi, psc_=psc_: e.matmul(
                 psc_, qkt[:, 1, pi, :], qkt[:, 0, pi, :], start=True, stop=True),
             waits=[P.w("a_bte", 2 * (blk + 1)), P.w("a_bex", max(0, blk - 1))],
             sig="a_bsc")
        P.op("scalar",
             lambda e, pi=pi, psc_=psc_: e.activation(
                 pb[:, pi, :], psc_, AF.Exp, scale=0.125, bias=mneg[:, 0:1]),
             waits=[P.w("a_bsc", blk + 1), e_mb,
                    P.w("a_bav", 2 * max(0, blk - 1))],
             sig="a_bex")
        P.op("vector",
             lambda e, pi=pi: e.tensor_tensor(pb[:, pi, :], pb[:, pi, :], bdm[:],
                                              OP.mult),
             waits=[P.w("a_bex", blk + 1), e_bdm], sig="a_bmk")
        psav = [PSD, PSE][pi]
        P.op("tensor",
             lambda e, pi=pi, psav=psav: e.matmul(
                 psav[:, 0:64], pb[:, pi, :], vbb[:, pi, :], start=True, stop=True),
             waits=[P.w("a_bmk", blk + 1), (f"a_vl{pi}", 16 * (blk // 2 + 1)),
                    P.w("a_brc", max(0, blk - 1)), P.w("a_bev", max(0, blk - 1))],
             sig="a_bav")
        P.op("tensor",
             lambda e, pi=pi, psav=psav: e.matmul(
                 psav[:, 64:65], pb[:, pi, :], ones_bf[:, 0:1],
                 start=True, stop=True),
             sig="a_bav")
        P.op("vector",
             lambda e, pi=pi, psav=psav: e.reciprocal(rden[:, pi:pi + 1],
                                                      psav[:, 64:65]),
             waits=[P.w("a_bav", 2 * (blk + 1))], sig="a_brc")
        P.op("vector",
             lambda e, pi=pi, psav=psav: e.tensor_scalar_mul(
                 attb[:, pi, :], psav[:, 0:64], rden[:, pi:pi + 1]),
             waits=[P.w("a_brc", blk + 1), (f"a_bsp{pi}", 16 * (blk // 2))],
             sig="a_bev")
        P.op("gpsimd",
             lambda e, blk=blk, pi=pi: e.dma_start(
                 out=af[blk * 128:(blk + 1) * 128, :], in_=attb[:, pi, :]),
             waits=[P.w("a_bev", blk + 1)], sig=f"a_bsp{pi}", dma=True)
    if maxstage == "ab":
        return finish_any([P.ev("a_bsp0"), P.ev("a_bsp1")], bdm, [es_a, es_m])

    wo_sb = sb("wo_sb", [128, 8, D], BF16, stack=es_a)
    atok = sb("atok", [128, 2, D], F32, stack=es_a)
    dmaop("sync", wo_sb[:].rearrange("p g m -> p (g m)"), wo_sp.ap()
          .rearrange("p g m -> p (g m)"), "a_wor", waits=[P.ev("a_wos")])
    e_wor = P.ev("a_wor")
    tpso = [PSC[:, 0:128], PSC[:, 512:640], PSD[:, 0:128], PSE[:, 0:128]]
    for ti in range(NTT):
        P.op("gpsimd",
             lambda e, ti=ti: e.dma_start(
                 out=atok[:, ti % 2, :],
                 in_=att_sp.ap()[ti * 128:(ti + 1) * 128, :]),
             waits=[P.w("a_otp", 8 * max(0, ti - 1)),
                    ("a_bsp0", P.n("a_bsp0")), ("a_bsp1", P.n("a_bsp1"))],
             sig=f"a_or{ti % 2}", dma=True)
        for kt in range(8):
            w = [(f"a_or{ti % 2}", 16 * (ti // 2 + 1)),
                 P.w("a_ote", max(0, 8 * ti + kt - 3))]
            if ti == 0:
                w += [P.w("a_brc", NBLK), P.w("a_bev", NBLK),
                      P.w("a_bex", NBLK)]
            P.op("tensor",
                 lambda e, ti=ti, kt=kt: e.transpose(
                     tpso[kt % 4], atok[:, ti % 2, kt * 128:(kt + 1) * 128],
                     id_bf[:]),
                 waits=w, sig="a_otp")
            P.op("vector",
                 lambda e, ti=ti, kt=kt: e.tensor_copy(attr[:, ti % 2, kt, :],
                                                       tpso[kt % 4]),
                 waits=[P.ev("a_otp"), P.w("a_om", max(0, ti - 1))], sig="a_ote")

        def mmo(e, ti=ti):
            ins = None
            psf = [PSA, PSB][ti % 2]
            for chk in range(2):
                for kt in range(8):
                    e.matmul(psf[:, chk * 512:(chk + 1) * 512],
                             attr[:, ti % 2, kt, :],
                             wo_sb[:, kt, chk * 512:(chk + 1) * 512],
                             start=(kt == 0), stop=False)
                ins = e.matmul(psf[:, chk * 512:(chk + 1) * 512],
                               ones_bf[0:1, :], borow[:, chk * 512:(chk + 1) * 512],
                               start=False, stop=True)
            return ins
        w = [P.w("a_ote", 8 * (ti + 1)), e_wor]
        if ti == 0:
            w.append(P.w("a_bte", 2 * NBLK))
        if ti >= 2:
            w.append(P.w("a_oe", ti - 1))
        P.op("tensor", mmo, waits=w, sig="a_om")
        P.op("vector",
             lambda e, ti=ti: e.tensor_add(stream[:, ti, :], stream[:, ti, :],
                                           [PSA, PSB][ti % 2][:]),
             waits=[P.ev("a_om")], sig="a_oe")
    e_x4 = P.ev("a_oe")
    if dbg:
        for ti in range(NTT):
            dmaop("gpsimd", dbg_t[2][ti * 128:(ti + 1) * 128, :], stream[:, ti, :],
                  "dbg_d", waits=[e_x4])

    es_a.close()
    if maxstage == 3:
        return finish(e_x4)

    P.fence([e_x4])
    P.release(["a_"])
    e_x5 = moe(1, e_x4)
    if dbg:
        for ti in range(NTT):
            dmaop("gpsimd", dbg_t[3][ti * 128:(ti + 1) * 128, :], stream[:, ti, :],
                  "dbg_d", waits=[e_x5])
    for ti in range(NTT):
        dmaop("sync", out_t[ti * 128:(ti + 1) * 128, :], stream[:, ti, :],
              "out_d", waits=[e_x5])
    fin = [P.ev("out_d")] + ([P.ev("dbg_d")] if dbg else [])
    P.op("vector", lambda e: e.engine_nop(), waits=fin)

    P.emit_all(block)
    es.close()
    return nc


def kernel(**inputs):
    dbg = os.environ.get("KDBG", "") == "1"
    import ml_dtypes
    ii = {k: np.asarray(v) for k, v in inputs.items()}
    x = ii["L0_attn_a_x"]
    Wup = ii["L0_attn_b_Wup"].reshape(DLAT, H, 2, DH)
    bup = ii["L0_attn_b_bup"].reshape(H, 2, DH)

    t_all = np.arange(T, dtype=np.float64)
    inv = 1.0 / (10000.0 ** (np.arange(0, DH, 2, dtype=np.float64) / DH))
    frq = t_all[:, None] * inv[None, :]
    ropek = np.concatenate([np.tile(np.cos(frq), (1, 16)),
                            np.tile(np.sin(frq), (1, 16))], 1).astype(np.float32)
    bdmask = np.kron(np.eye(8), np.ones((16, 16))).astype(np.float32)

    in_maps = []
    for c in range(8):
        b, q = divmod(c, 4)
        sl = slice(q * DIL, (q + 1) * DIL)
        cols = np.r_[q * DIL:(q + 1) * DIL, 2048 + q * DIL:2048 + (q + 1) * DIL]
        m = {
            "x_b": np.ascontiguousarray(x[b]),
            "x_tr": np.ascontiguousarray(x[b, q * TR:(q + 1) * TR]),
            "in_w_l": np.ascontiguousarray(ii["L0_attn_a_in_w"][:, :, cols]),
            "conv_w_l": np.ascontiguousarray(ii["L0_attn_a_conv_w"][:, sl].reshape(2, 4, 128, DCONV).transpose(2, 0, 1, 3)),
            "conv_b_l": np.ascontiguousarray(ii["L0_attn_a_conv_b"][:, sl].reshape(2, 4, 128).transpose(2, 0, 1).reshape(128, 8)),
            "xp_w_l": np.ascontiguousarray(ii["L0_attn_a_xp_w"][:, sl]),
            "dt_w_l": np.ascontiguousarray(ii["L0_attn_a_dt_w"][:, :, sl]),
            "dt_b_l": np.ascontiguousarray(ii["L0_attn_a_dt_b"][:, sl].reshape(2, 4, 128).transpose(2, 0, 1).reshape(128, 8)),
            "A_log_l": np.ascontiguousarray(ii["L0_attn_a_A_log"][:, sl].reshape(2, 4, 128, NST).transpose(2, 0, 1, 3)),
            "D_l": np.ascontiguousarray(ii["L0_attn_a_D"][:, sl].reshape(2, 4, 128).transpose(2, 0, 1).reshape(128, 8)),
            "out_wT_l": np.ascontiguousarray(
                ii["L0_attn_a_out_w"][:, sl, :].transpose(0, 2, 1)),
            "proj_w_d": np.ascontiguousarray(
                ii["L0_attn_a_proj_w"].reshape(2, D, D)),
            "proj_b": ii["L0_attn_a_proj_b"].reshape(1, D),
            "Wq": ii["L0_attn_b_Wq"],
            "bq": ii["L0_attn_b_bq"].reshape(1, D),
            "Wdown": ii["L0_attn_b_Wdown"],
            "bdown": ii["L0_attn_b_bdown"].reshape(1, DLAT),
            "Wup_k": np.ascontiguousarray(Wup[:, :, 0].reshape(DLAT, D)),
            "Wup_v": np.ascontiguousarray(Wup[:, :, 1].reshape(DLAT, D)),
            "bup_k": np.ascontiguousarray(bup[:, 0].reshape(1, D)),
            "bup_v": np.ascontiguousarray(bup[:, 1].reshape(1, D)),
            "Wout": ii["L0_attn_b_Wout"],
            "bout": ii["L0_attn_b_bout"].reshape(1, D),
            "centT": np.ascontiguousarray(ii["L0_ffn_centroids"].T),
            "eb": ii["L0_ffn_expert_biases"].reshape(1, NE),
            "WrT": np.ascontiguousarray(ii["L0_ffn_Wr"].transpose(0, 2, 1)),
            "br": ii["L0_ffn_br"],
            "WsT": np.ascontiguousarray(ii["L0_ffn_Ws"].transpose(0, 2, 1)),
            "bs": ii["L0_ffn_bs"],
            "identity": np.eye(128, dtype=np.float32),
            "ropeq_tok": np.ascontiguousarray(ropek[q * TR:(q + 1) * TR]),
            "bdmask": bdmask,
        }
        in_maps.append(m)

    nc = build_program(dbg)
    res = run_bass_kernel_spmd(nc, in_maps, core_ids=list(range(8)))
    out = np.empty((2, T, D), np.float32)
    for c in range(8):
        b, q = divmod(c, 4)
        out[b, q * TR:(q + 1) * TR] = res.results[c]["out"]
    if dbg:
        kernel._dbg = [
            np.stack([np.concatenate([res.results[b * 4 + q][f"dbg{i}"]
                                      for q in range(4)], 0) for b in range(2)])
            for i in range(4)]
    return out


# version 02 (final)
# speedup vs baseline: 1.0102x; 1.0102x over previous
import os
import numpy as np
from contextlib import ExitStack

import concourse.bass as bass
import concourse.mybir as mybir
from concourse.bass_utils import run_bass_kernel_spmd

F32 = mybir.dt.float32
BF16 = mybir.dt.bfloat16
AF = mybir.ActivationFunctionType
OP = mybir.AluOpType
AXX = mybir.AxisListType.X

B, T, D = 2, 8192, 1024
DIL = 512
NST, DCONV, RNK = 16, 4, 64
H, DH, DLAT = 16, 64, 256
NE = 16
TR = 2048
NTT = TR // 128
MCH = 1024
NCH = T // MCH
GROUPS = [[0, 1, 2, 3], [4, 5, 6, 7]]
LNEPS = 1e-5
ENGS = ("sync", "tensor", "vector", "scalar", "gpsimd")


class Prog:
    def __init__(self, nc):
        self.nc = nc
        self.ops = []
        self.sems = {}
        self.counts = {}
        self.pool = []
        self.closed = set()
        self.dma_tags = set()
        self.bases = {}
        self.pending_fence = {e: [] for e in ENGS}

    def sem(self, name):
        if name not in self.sems:
            assert name not in self.closed, f"sem {name} used after release"
            if self.pool and name not in self.dma_tags:
                handle, base = self.pool.pop()
            else:
                handle, base = self.nc.semaphore(name).__enter__(), 0
            self.sems[name] = handle
            self.counts[name] = base
            self.bases[name] = base
        return self.sems[name]

    def release(self, prefixes):
        for name in list(self.sems):
            if name in self.closed:
                continue
            if any(name.startswith(p) for p in prefixes):
                if name in self.dma_tags:
                    continue
                self.pool.append((self.sems[name], self.counts[name]))
                self.closed.add(name)

    def op(self, eng, emit, waits=(), sig=None, dma=False):
        sigs = []
        if sig:
            if isinstance(sig, str):
                sig = [sig]
            for s in sig:
                if isinstance(s, str):
                    s = (s, 16 if dma else 1)
                assert s[0] not in self.closed, f"sig on released sem {s[0]}"
                if dma:
                    assert s[0] not in self.bases or self.bases[s[0]] == 0, \
                        f"dma tag {s[0]} grabbed a pooled sem; pre-declare it"
                    self.dma_tags.add(s[0])
                self.sem(s[0])
                self.counts[s[0]] += s[1]
                sigs.append(s)
        w = [x for x in waits if x is not None and x[1] > 0]
        if self.pending_fence[eng]:
            w = self.pending_fence[eng] + w
            self.pending_fence[eng] = []
        for x in w:
            self.sem(x[0])
        self.ops.append((eng, emit, w, sigs))

    def ev(self, semname):
        self.sem(semname)
        return (semname, self.counts[semname])

    def n(self, semname):
        self.sem(semname)
        return self.counts[semname]

    def w(self, semname, v):
        self.sem(semname)
        return (semname, self.bases[semname] + v)

    def fence(self, evs):
        evs = [x for x in evs if x is not None and x[1] > 0]
        for e in ENGS:
            self.pending_fence[e] = list(evs)

    def emit_all(self, block):
        by = {e: [] for e in ENGS}
        for (eng, emit, waits, sigs) in self.ops:
            by[eng].append((emit, waits, sigs))

        chain = {e: self.nc.semaphore(f"chain_{e}").__enter__()
                 for e in ("vector", "scalar", "gpsimd")}
        _NOCHAIN = ("dma_start", "collective_compute", "wait_ge",
                    "indirect_dma_start")

        class ChainedEngine:

            def __init__(self, eng, csem):
                self._e = eng
                self._c = csem
                self._n = 0
                self._last = None
                self._point = None

            def _flush_last(self):
                if self._last is not None:
                    self._n += 1
                    self._last.then_inc(self._c, 1)
                    self._point = (self._c, self._n)
                    self._last = None

            def __getattr__(self, name):
                f = getattr(self._e, name)
                if not callable(f) or name.startswith("_") or name in _NOCHAIN:
                    return f
                proxy = self

                def wrap(*a, **k):
                    proxy._flush_last()
                    if proxy._point is not None:
                        proxy._e.wait_ge(proxy._point[0], proxy._point[1])
                    ins = f(*a, **k)
                    proxy._last = ins
                    return ins
                return wrap

        def runner(name, lst):
            csem = chain.get(name)

            def run(engine):
                eng = ChainedEngine(engine, csem) if csem is not None else engine
                run_counts = {}
                for (emit, waits, sigs) in lst:
                    seen = set()
                    for (s, v) in waits:
                        if (s, v) in seen:
                            continue
                        seen.add((s, v))
                        engine.wait_ge(self.sems[s], v)
                    ins = emit(eng)
                    if csem is None:
                        for (s, k) in sigs:
                            ins.then_inc(self.sems[s], k)
                    elif sigs:
                        assert len(sigs) == 1, "chained engines: one sig per op"
                        s, k = sigs[0]
                        if s not in run_counts:
                            run_counts[s] = self.bases.get(s, 0)
                        run_counts[s] += k
                        if eng._last is ins:
                            ins.then_inc(self.sems[s], k)
                            eng._point = (self.sems[s], run_counts[s])
                            eng._last = None
                        else:
                            ins.then_inc(self.sems[s], k)
            return run

        for name in ENGS:
            if by[name]:
                getattr(block, name)(runner(name, by[name]))


def build_program(dbg, maxstage=4):
    nc = bass.Bass("TRN2", target_bir_lowering=False)
    es = ExitStack()

    def inp(name, shape, dt=F32):
        return nc.dram_tensor(name, list(shape), dt, kind="ExternalInput").ap()

    x_b = inp("x_b", (T, D))
    x_tr = inp("x_tr", (TR, D))
    in_w_l = inp("in_w_l", (2, D, 2 * DIL))
    conv_w_l = inp("conv_w_l", (128, 8, DCONV))
    conv_b_l = inp("conv_b_l", (128, 8))
    xp_w_l = inp("xp_w_l", (2, DIL, 96))
    dt_w_l = inp("dt_w_l", (2, RNK, DIL))
    dt_b_l = inp("dt_b_l", (128, 8))
    A_log_l = inp("A_log_l", (128, 8, NST))
    D_l = inp("D_l", (128, 8))
    out_wT_l = inp("out_wT_l", (2, D, DIL))
    proj_w_d = inp("proj_w_d", (2, D, D))
    proj_b = inp("proj_b", (1, D))
    Wq_i = inp("Wq", (D, D))
    bq_i = inp("bq", (1, D))
    Wdown_i = inp("Wdown", (D, DLAT))
    bdown_i = inp("bdown", (1, DLAT))
    Wup_k_i = inp("Wup_k", (DLAT, D))
    Wup_v_i = inp("Wup_v", (DLAT, D))
    bup_k_i = inp("bup_k", (1, D))
    bup_v_i = inp("bup_v", (1, D))
    Wout_i = inp("Wout", (D, D))
    bout_i = inp("bout", (1, D))
    centT_i = inp("centT", (D, NE))
    eb_i = inp("eb", (1, NE))
    WrT_i = inp("WrT", (NE, D, D))
    br_i = inp("br", (NE, D))
    WsT_i = inp("WsT", (2, D, D))
    bs_i = inp("bs", (2, D))
    identity_i = inp("identity", (128, 128), F32)
    ropeq_tok = inp("ropeq_tok", (TR, 2 * DIL))
    bdmask_i = inp("bdmask", (128, 128))

    out_t = nc.dram_tensor("out", [TR, D], F32, kind="ExternalOutput").ap()
    dbg_t = [nc.dram_tensor(f"dbg{i}", [TR, D], F32, kind="ExternalOutput").ap()
             for i in range(4)] if dbg else None

    dram = nc.dram_tensor
    dbc_in = dram("dbc_in", [2, 96, T], F32)
    dbc_out = dram("dbc_out", [2, 96, T], F32)
    bc_rows = dram("bc_rows", [2, 32, T], BF16)
    xc_sp = dram("xc_sp", [2, 4, 128, T], BF16)
    sz_sp = dram("sz_sp", [2, 4, 128, T], BF16)
    yf_sp = dram("yf_sp", [4, 128, T], BF16)
    mout_in = dram("mout_in", [T, D], F32)
    mout_rs = dram("mout_rs", [TR, D], F32)
    q_sp = dram("q_sp", [TR, D], BF16)
    k_sp = dram("k_sp", [TR, D], BF16)
    v_sp = dram("v_sp", [TR, D], BF16)
    att_sp = dram("att_sp", [TR, D], BF16)
    wr_bf_sp = dram("wr_bf_sp", [NE, 128, 8, D], BF16)
    wsh_sp = dram("wsh_sp", [128, 8, D], BF16)
    br_hb = dram("br_hb", [NE + 1, D], BF16)
    wo_sp = dram("wo_sp", [128, 8, D], BF16)
    rk_bf_sp = dram("rk_bf_sp", [2, 128, T], BF16)
    mb_hb = dram("mb_hb", [1, 1], F32)

    def sb(name, shape, dt=F32, stack=None):
        return (stack or es).enter_context(nc.sbuf_tensor(name, list(shape), dt))

    id_bf = sb("id_bf", [128, 128], F32)
    ones_bf = sb("ones_bf", [128, 128], BF16)
    lnst = sb("lnst", [128, 12])
    lnmv = sb("lnmv", [128, 4])
    epsc = sb("epsc", [128, 2])

    PSA = es.enter_context(nc.psum_tensor("PSA", [128, 1024], F32))
    PSB = es.enter_context(nc.psum_tensor("PSB", [128, 1024], F32))
    PSC = es.enter_context(nc.psum_tensor("PSC", [128, 1024], F32))
    PSD = es.enter_context(nc.psum_tensor("PSD", [128, 512], F32))
    PSE = es.enter_context(nc.psum_tensor("PSE", [128, 512], F32))
    SLOT6 = [PSA[:, 0:512], PSA[:, 512:1024], PSB[:, 0:512], PSB[:, 512:1024],
             PSC[:, 0:512], PSC[:, 512:1024]]
    TPS = [PSA[:, 0:128], PSA[:, 512:640], PSB[:, 0:128], PSB[:, 512:640],
           PSC[:, 0:128], PSC[:, 512:640], PSD[:, 0:128], PSE[:, 0:128]]

    block = es.enter_context(nc.Block())
    P = Prog(nc)
    P.dma_tags.update(["s3b_od0", "s3b_od1", "s3a_ysp", "s1_sp", "s1_zsp",
                       "a_qsp0", "a_qsp1", "a_qsp2", "a_qsp3", "a_ksp0", "a_ksp1", "a_ksp2",
                       "a_ksp3", "a_asp0", "a_asp1", "a_wos", "a_ld0", "a_ld1",
                       "a_rw", "a_or0", "a_or1", "s3x_b0", "s3x_b1", "s3x_c0",
                       "s3x_c1", "me_shl", "me_brl0", "me_brl1", "a_wl", "a_bl0", "a_bl1",
                       "a_bsp0", "a_bsp1", "a_vsp0", "a_vsp1", "a_bm", "a_vl0", "a_vl1", "m0_ws0", "m0_ws1", "mw_sp", "out_d", "dbg_d"])

    def _init_consts(e):
        e.memset(epsc[:, 0:1], LNEPS)
        return e.memset(epsc[:, 1:2], 1e-30)
    P.op("vector", _init_consts, sig="c_init")
    e_const = P.ev("c_init")

    def dmaop(eng, out_ap, in_ap, sig, waits=()):
        P.op(eng, lambda e, o=out_ap, i=in_ap: e.dma_start(out=o, in_=i),
             waits=waits, sig=sig, dma=True)

    def ln_tile(tag, src_ap, dst_ap, dst_waits=()):
        sv, ss = tag + "_v", tag + "_s"

        def v1(e, src=src_ap):
            e.bn_stats(lnst[:, 0:6], src[:, 0:512])
            e.bn_stats(lnst[:, 6:12], src[:, 512:1024])
            e.bn_aggr(lnmv[:, 0:2], lnst[:, 0:12])
            return e.tensor_copy(lnmv[:, 2:3], lnmv[:, 1:2])
        P.op("vector", v1, sig=sv)

        def s1(e):
            e.activation(lnmv[:, 2:3], lnmv[:, 2:3], AF.Ln, bias=epsc[:, 0:1])
            return e.activation(lnmv[:, 2:3], lnmv[:, 2:3], AF.Exp, scale=-0.5)
        P.op("scalar", s1, waits=[P.ev(sv), e_const], sig=ss)
        P.op("vector",
             lambda e, src=src_ap, dst=dst_ap: e.tensor_scalar(
                 dst, src, lnmv[:, 0:1], lnmv[:, 2:3], OP.subtract, OP.mult),
             waits=[P.ev(ss)] + list(dst_waits), sig=sv)
        return P.ev(sv)

    es_m = ExitStack()
    es_m1 = ExitStack()

    es_s0 = ExitStack()
    convw = sb("convw", [128, 8, DCONV], stack=es_m)
    convb = sb("convb", [128, 8], stack=es_m)
    dtb = sb("dtb", [128, 8], stack=es_m)
    Dp = sb("Dp", [128, 8], stack=es_m)
    A_f = sb("A_f", [128, 8, NST], stack=es_m)
    xpw_bf = sb("xpw_bf", [128, 8, 96], BF16, stack=es_m)
    dtw_bf = sb("dtw_bf", [128, DIL], BF16, stack=es_m)
    weff = sb("weff", [128, 8, D], BF16, stack=es_m)
    dbcT = sb("dbcT", [128, 2, T], stack=es_m1)

    inw_bf = sb("inw_bf", [128, 16, D], BF16, stack=es_m1)
    stg_big = sb("stg_big", [128, 8, D], stack=es_s0)
    stg_bf = sb("stg_bf", [128, 8, D], BF16, stack=es_s0)

    dmaop("sync", id_bf[:], identity_i, "s0_ld")
    dmaop("sync", convw[:], conv_w_l, "s0_ld")
    dmaop("sync", convb[:], conv_b_l, "s0_ld")
    dmaop("sync", dtb[:], dt_b_l, "s0_ld")
    dmaop("sync", Dp[:], D_l, "s0_ld")
    dmaop("sync", A_f[:], A_log_l, "s0_ld")
    e_s0 = P.ev("s0_ld")
    P.op("vector", lambda e: e.memset(ones_bf[:], 1.0), sig="s0_v")
    P.op("scalar", lambda e: e.activation(A_f[:], A_f[:], AF.Exp),
         waits=[e_s0], sig="s0_s")
    P.op("vector", lambda e: e.tensor_scalar_mul(A_f[:], A_f[:], -1.0),
         waits=[P.ev("s0_s")], sig="s0_v")
    e_A = P.ev("s0_v")

    stg_flat = stg_big[:].rearrange("p a m -> p (a m)")

    def load_convert(dst_bf_ap, hbm_ap, ngrp, width):
        v = stg_flat[:, 0:ngrp * width].rearrange("p (g w) -> p g w", g=ngrp)
        dmaop("sync", v, hbm_ap, "lc_ld", waits=[P.ev("lc_cv")])
        P.op("vector",
             lambda e, d=dst_bf_ap, v=v: e.tensor_copy(d, v),
             waits=[P.ev("lc_ld")], sig="lc_cv")
        return P.ev("lc_cv")

    inw_r = in_w_l.rearrange("r (kt p) m -> p (r kt) m", p=128)
    load_convert(inw_bf[:, 0:8, :], inw_r[:, 0:8, :], 8, D)
    load_convert(inw_bf[:, 8:16, :], inw_r[:, 8:16, :], 8, D)
    load_convert(xpw_bf[:], xp_w_l.rearrange("r (kt p) m -> p (r kt) m", p=128), 8, 96)
    load_convert(dtw_bf[:].rearrange("p (a m) -> p a m", a=1),
                 dt_w_l.rearrange("r k m -> (r k) m").rearrange("p (a m) -> p a m", a=1),
                 1, DIL)
    e_w0 = P.ev("lc_cv")

    pw_bf = sb("pw_bf", [128, 16, D], BF16, stack=es_s0)
    stgv = stg_bf[:].rearrange("p a m -> p (a m)").rearrange(
        "p (g w) -> p g w", g=16)
    owr = out_wT_l.rearrange("r (kt p) m -> p (r kt) m", p=128)
    e_ow = load_convert(stgv, owr, 16, DIL)
    pwr = proj_w_d.rearrange("r (kt p) m -> p (r kt) m", p=128)
    load_convert(pw_bf[:, 0:8, :], pwr[:, 0:8, :], 8, D)
    e_pw = load_convert(pw_bf[:, 8:16, :], pwr[:, 8:16, :], 8, D)
    gi = 0
    for dir_ in range(2):
        for rt in range(4):
            for nchk in range(2):
                ps = SLOT6[gi % 4]

                def mm(e, dir_=dir_, rt=rt, nchk=nchk, ps=ps):
                    for kt in range(8):
                        ins = e.matmul(ps,
                                       stgv[:, dir_ * 8 + kt, rt * 128:(rt + 1) * 128],
                                       pw_bf[:, dir_ * 8 + kt, nchk * 512:(nchk + 1) * 512],
                                       start=(kt == 0), stop=(kt == 7))
                    return ins
                w = [e_ow, e_pw]
                if gi >= 4:
                    w.append(("weff_e", gi - 3))
                P.op("tensor", mm, waits=w, sig="weff_m")
                P.op("vector",
                     lambda e, dir_=dir_, rt=rt, nchk=nchk, ps=ps: e.tensor_copy(
                         weff[:, dir_ * 4 + rt, nchk * 512:(nchk + 1) * 512], ps),
                     waits=[P.ev("weff_m")], sig="weff_e")
                gi += 1
    e_weff = P.ev("weff_e")
    es_s0.close()

    xstage = sb("xstage", [128, 2, D], stack=es_m1)
    xhat_m = sb("xhat_m", [128, 2, D], stack=es_m1)
    xhat1T = sb("xhat1T", [128, 8, MCH], BF16, stack=es_m1)
    xibufA = sb("xibufA", [128, 8, MCH + 3], BF16, stack=es_m1)
    xibufB = sb("xibufB", [128, 8, MCH + 3], BF16, stack=es_m1)
    cacc = sb("cacc", [128, 2, MCH], stack=es_m1)
    xcc = sb("xcc", [128, 4, MCH], BF16, stack=es_m1)
    szc = sb("szc", [128, 4, MCH], BF16, stack=es_m1)

    P.op("vector", lambda e: e.memset(dbcT[:], 0.0), sig="s1_dbc0")
    e_dbc0 = P.ev("s1_dbc0")
    NXT = MCH // 128

    def conv_dir(c_conv, dir_, base, buf):
        e_prev_sp = P.ev("s1_sp")
        e_prev_xp = P.ev("s1_xpe")
        for dtl in range(4):
            def cv(e, dtl=dtl, base=base, buf=buf, dir_=dir_):
                ins = None
                for k in range(4):
                    wi = k if dir_ == 0 else 3 - k
                    s_ap = buf[:, base + dtl, k:k + MCH]
                    wc = convw[:, dir_ * 4 + dtl, wi:wi + 1]
                    if k == 0:
                        ins = e.tensor_scalar_mul(cacc[:, 0, :], s_ap, wc)
                    else:
                        ins = e.scalar_tensor_tensor(cacc[:, k % 2, :], s_ap, wc,
                                                     cacc[:, (k + 1) % 2, :],
                                                     OP.mult, OP.add)
                return ins
            P.op("vector", cv, waits=[P.ev("s1_xc")], sig="s1_cv")
            P.op("scalar",
                 lambda e, dtl=dtl, dir_=dir_: e.activation(
                     xcc[:, dtl, :], cacc[:, 1, :], AF.Silu,
                     bias=convb[:, dir_ * 4 + dtl:dir_ * 4 + dtl + 1]),
                 waits=[P.ev("s1_cv"), e_prev_sp, e_prev_xp], sig="s1_xc")
            P.op("gpsimd",
                 lambda e, dtl=dtl, dir_=dir_, c_conv=c_conv: e.dma_start(
                     out=xc_sp.ap()[dir_, dtl, :, c_conv * MCH:(c_conv + 1) * MCH],
                     in_=xcc[:, dtl, :]),
                 waits=[P.ev("s1_xc")], sig="s1_sp", dma=True)
        e_xc = P.ev("s1_xc")
        for nchk in range(2):
            ps = [PSD, PSE][nchk]

            def mm(e, dir_=dir_, nchk=nchk, ps=ps):
                for kt in range(4):
                    ins = e.matmul(ps[0:96, :], xpw_bf[:, dir_ * 4 + kt, :],
                                   xcc[:, kt, nchk * 512:(nchk + 1) * 512],
                                   start=(kt == 0), stop=(kt == 3))
                return ins
            P.op("tensor", mm, waits=[e_xc, e_w0, P.ev("s1_xpe")], sig="s1_xp")
            P.op("vector",
                 lambda e, dir_=dir_, nchk=nchk, ps=ps, c_conv=c_conv: e.tensor_add(
                     dbcT[0:96, dir_, c_conv * MCH + nchk * 512:c_conv * MCH + (nchk + 1) * 512],
                     dbcT[0:96, dir_, c_conv * MCH + nchk * 512:c_conv * MCH + (nchk + 1) * 512],
                     ps[0:96, :]),
                 waits=[P.ev("s1_xp"), e_dbc0], sig="s1_xpe")

    slot6_guard = [None] * 6
    for c in range(NCH):
        pA, pP = (c % 2) * 4, ((c + 1) % 2) * 4
        e_prev_evx = P.ev("s1_evx")
        e_prev_evz = P.ev("s1_evz")
        e_prev_xpe2 = P.ev("s1_xpe")
        for i in range(NXT):
            k = c * NXT + i
            w = [e_w0, e_weff] if k < 2 else [("s1_ln_v", 2 * (k - 1))]
            dmaop("sync", xstage[:, k % 2, :],
                  x_b[c * MCH + 128 * i: c * MCH + 128 * (i + 1), :],
                  f"s1_x{k % 2}", waits=w)
            P.op("vector", lambda e: e.engine_nop(),
                 waits=[(f"s1_x{k % 2}", 16 * (k // 2 + 1))])
            dwk = [] if k < 2 else [("s1_tp", 8 * (k - 1))]
            lev = ln_tile("s1_ln", xstage[:, k % 2, :], xhat_m[:, k % 2, :],
                          dst_waits=dwk)
            for j in range(8):
                extra = [lev, e_prev_evx, e_prev_evz, e_prev_xpe2]
                if c == 0 and k == 0:
                    extra.append(e_weff)
                if k > 0:
                    extra.append(("s1_tpe", 8 * k))
                P.op("tensor",
                     lambda e, j=j, k=k: e.transpose(
                         TPS[j], xhat_m[:, k % 2, j * 128:(j + 1) * 128], id_bf[:]),
                     waits=extra, sig="s1_tp")
                P.op("vector",
                     lambda e, j=j, i=i: e.tensor_copy(
                         xhat1T[:, j, i * 128:(i + 1) * 128], TPS[j]),
                     waits=[P.ev("s1_tp")], sig="s1_tpe")
        e_tp = P.ev("s1_tpe")
        gidx = 0
        for dir_ in range(2):
            for mt in range(8):
                for nchk in range(2):
                    ps = SLOT6[gidx % 6]

                    def mm(e, dir_=dir_, mt=mt, nchk=nchk, ps=ps):
                        for kt in range(8):
                            ins = e.matmul(ps,
                                           inw_bf[:, dir_ * 8 + kt, mt * 128:(mt + 1) * 128],
                                           xhat1T[:, kt, nchk * 512:(nchk + 1) * 512],
                                           start=(kt == 0), stop=(kt == 7))
                        return ins
                    w = [e_tp, e_w0]
                    if slot6_guard[gidx % 6] is not None:
                        w.append(slot6_guard[gidx % 6])
                    P.op("tensor", mm, waits=w, sig="s1_mm")
                    if mt < 4:
                        buf = xibufA if dir_ == 0 else xibufB
                        off = 3 if dir_ == 0 else 0
                        P.op("vector",
                             lambda e, buf=buf, off=off, mt=mt, nchk=nchk, ps=ps, pA=pA:
                             e.tensor_copy(
                                 buf[:, pA + mt, off + nchk * 512: off + (nchk + 1) * 512], ps),
                             waits=[P.ev("s1_mm")], sig="s1_evx")
                        slot6_guard[gidx % 6] = P.ev("s1_evx")
                    else:
                        zguard = ("s1_zsp", 64 * (2 * c + dir_))
                        P.op("scalar",
                             lambda e, mt=mt, nchk=nchk, ps=ps:
                             e.activation(szc[:, mt - 4, nchk * 512:(nchk + 1) * 512],
                                          ps, AF.Silu),
                             waits=[P.ev("s1_mm"), zguard], sig="s1_evz")
                        slot6_guard[gidx % 6] = P.ev("s1_evz")
                    gidx += 1
            e_zev = P.ev("s1_evz")
            for g in range(4):
                P.op("gpsimd",
                     lambda e, g=g, c=c, dir_=dir_: e.dma_start(
                         out=sz_sp.ap()[dir_, g, :, c * MCH:(c + 1) * MCH],
                         in_=szc[:, g, :]),
                     waits=[e_zev], sig="s1_zsp", dma=True)
        e_ev = [P.ev("s1_evx"), P.ev("s1_evz")]

        def pads(e, c=c, pA=pA, pP=pP):
            if c == 0:
                ins = e.memset(xibufA[:, pA:pA + 4, 0:3], 0.0)
            else:
                ins = e.tensor_copy(xibufA[:, pA:pA + 4, 0:3],
                                    xibufA[:, pP:pP + 4, MCH:MCH + 3])
            if c > 0:
                ins = e.tensor_copy(xibufB[:, pP:pP + 4, MCH:MCH + 3],
                                    xibufB[:, pA:pA + 4, 0:3])
            return ins
        P.op("vector", pads, waits=e_ev, sig="s1_pad")
        conv_dir(c, 0, pA, xibufA)
        if c > 0:
            conv_dir(c - 1, 1, pP, xibufB)
        if c == NCH - 1:
            P.op("vector",
                 lambda e, pA=pA: e.memset(xibufB[:, pA:pA + 4, MCH:MCH + 3], 0.0),
                 sig="s1_pad")
            conv_dir(NCH - 1, 1, pA, xibufB)

    e_front = [P.ev("s1_xpe"), P.ev("s1_sp"), P.ev("s1_zsp")]

    def finish_any(e_last, src, stacks):
        nblk = max(1, src.shape[-1] // 1024)
        for ti in range(NTT):
            dmaop("sync", out_t[ti * 128:(ti + 1) * 128, :],
                  src[:, (ti % nblk) * 1024:(ti % nblk + 1) * 1024], "out_d",
                  waits=e_last)
        P.op("vector", lambda e: e.engine_nop(), waits=[P.ev("out_d")])
        P.emit_all(block)
        for s in stacks:
            s.close()
        es.close()
        return nc
    if maxstage == "s1":
        return finish_any(e_front, dbcT[:, 0, :], [es_m1, es_m])
    if maxstage == "s1x":
        for ti in range(NTT):
            dmaop("gpsimd", out_t[ti * 128:(ti + 1) * 128, :],
                  xc_sp.ap()[0, ti // 4, :, (ti % 4) * 1024:(ti % 4 + 1) * 1024],
                  "out_d", waits=e_front)
        P.op("vector", lambda e: e.engine_nop(), waits=[P.ev("out_d")])
        P.emit_all(block)
        es_m1.close(); es_m.close(); es.close()
        return nc
    for dir_ in range(2):
        P.op("gpsimd",
             lambda e, dir_=dir_: e.dma_start(out=dbc_in.ap()[dir_],
                                              in_=dbcT[0:96, dir_, :]),
             waits=e_front, sig="s2_b", dma=True)
    P.op("gpsimd",
         lambda e: e.collective_compute(
             "AllReduce", OP.add, replica_groups=GROUPS,
             ins=[dbc_in.ap().opt()], outs=[dbc_out.ap().opt()]),
         waits=[P.ev("s2_b")], sig="s2_cc")
    e_cc1 = P.ev("s2_cc")

    es_m1.close()
    dtrows = sb("dtrows", [128, T], BF16, stack=es_m)
    P.fence([e_cc1])
    with nc.sbuf_tensor("dtstg", [128, T], F32) as dtstg, \
            nc.sbuf_tensor("bcstg", [64, T], F32) as bcstg, \
            nc.sbuf_tensor("bcbf", [64, T], BF16) as bcbf:
        for dir_ in range(2):
            dmaop("sync", dtstg[dir_ * 64:(dir_ + 1) * 64, :],
                  dbc_out.ap()[dir_, 0:64, :], "s2_dtl", waits=[e_cc1])
            dmaop("sync", bcstg[dir_ * 32:(dir_ + 1) * 32, :],
                  dbc_out.ap()[dir_, 64:96, :], "s2_bc", waits=[e_cc1])
        P.op("vector", lambda e, dtstg=dtstg: e.tensor_copy(dtrows[:], dtstg[:]),
             waits=[P.ev("s2_dtl")], sig="s2_dt")
        P.op("vector", lambda e, bcstg=bcstg, bcbf=bcbf: e.tensor_copy(bcbf[:], bcstg[:]),
             waits=[P.ev("s2_bc")], sig="s2_cv")
        for dir_ in range(2):
            P.op("gpsimd",
                 lambda e, dir_=dir_, bcbf=bcbf: e.dma_start(
                     out=bc_rows.ap()[dir_], in_=bcbf[dir_ * 32:(dir_ + 1) * 32, :]),
                 waits=[P.ev("s2_cv")], sig="s2_bcw", dma=True)
    e_dt = P.ev("s2_dt")
    e_bc = P.ev("s2_bcw")
    if maxstage == "s2":
        for ti in range(NTT):
            dmaop("gpsimd", out_t[ti * 128:(ti + 1) * 128, :],
                  dtrows[:, (ti % 8) * 1024:(ti % 8 + 1) * 1024], "out_d",
                  waits=[e_dt, e_bc])
        P.op("vector", lambda e: e.engine_nop(), waits=[P.ev("out_d")])
        P.emit_all(block)
        es_m.close(); es.close()
        return nc
    P.fence([e_bc])
    P.release(["s0_", "lc_", "weff_", "s1_ln", "s1_x0", "s1_x1", "s1_tp", "s1_mm",
               "s1_evx", "s1_evz", "s1_cv", "s1_xc", "s1_pad", "s1_xp", "s1_dbc0",
               "s2_b", "s2_cc", "s2_bc", "s2_cv"])

    es_m3 = ExitStack()
    dtc = sb("dtc", [128, 4, MCH], stack=es_m3)
    xcr = sb("xcr", [128, 8, MCH], BF16, stack=es_m3)
    szr = sb("szr", [128, 8, MCH], BF16, stack=es_m3)
    ub = sb("ub", [128, 4, MCH], BF16, stack=es_m3)
    bexp = sb("bexp", [128, 2, MCH], BF16, stack=es_m3)
    cexp = sb("cexp", [128, 2, MCH], BF16, stack=es_m3)
    ubn = sb("ubn", [128, 2, MCH], BF16, stack=es_m3)
    dAb = sb("dAb", [128, 2, MCH], stack=es_m3)
    hb = sb("hb", [128, 2, MCH], BF16, stack=es_m3)
    chb = sb("chb", [128, 2, MCH], BF16, stack=es_m3)
    ycur = sb("ycur", [128, 4, MCH], BF16, stack=es_m3)
    yfr = sb("yfr", [128, 4, MCH], BF16, stack=es_m3)
    ygate = sb("ygate", [128, 8, MCH], BF16, stack=es_m3)
    outp = sb("outp", [128, 2, D], stack=es_m3)
    state = sb("state", [128, 256], stack=es_m3)

    P.op("vector", lambda e: e.memset(state[:], 0.0), sig="s3_i")
    e_st0 = P.ev("s3_i")

    def scan_chunks(dir_, rev, tag, other_sp, first_guards=(), reader_tag=None):
        chunks = list(range(NCH - 1, -1, -1)) if rev else list(range(NCH))
        for ci, c in enumerate(chunks):
            fg = list(first_guards) if ci == 0 else []
            rg = [(reader_tag, P.n(reader_tag))] if reader_tag else []
            base_u = P.n(tag + "_u")
            base_sp = P.n(tag + "_sp")
            gi2 = 0
            for mt in range(4):
                for nchk in range(2):
                    ps = [PSD, PSE][gi2 % 2]

                    def mm(e, mt=mt, nchk=nchk, ps=ps, c=c, dir_=dir_):
                        return e.matmul(
                            ps[:], dtw_bf[dir_ * 64:(dir_ + 1) * 64, mt * 128:(mt + 1) * 128],
                            dtrows[dir_ * 64:(dir_ + 1) * 64,
                                   c * MCH + nchk * 512:c * MCH + (nchk + 1) * 512],
                            start=True, stop=True)
                    w = [e_dt, e_w0]
                    if gi2 >= 2:
                        w.append((tag + "_sp", base_sp + gi2 - 1))
                    else:
                        w.append((tag + "_sp", base_sp))
                        if ci == 0 and other_sp is not None:
                            w.append(other_sp)
                    P.op("tensor", mm, waits=w, sig=tag + "_mm")

                    def spf(e, mt=mt, nchk=nchk, ps=ps, dir_=dir_):
                        dst = dtc[:, mt, nchk * 512:(nchk + 1) * 512]
                        e.activation(dst, ps[:], AF.Exp,
                                     bias=dtb[:, dir_ * 4 + mt:dir_ * 4 + mt + 1])
                        return e.activation(dst, dst, AF.Ln, bias=1.0)
                    P.op("scalar", spf,
                         waits=[P.ev(tag + "_mm"), (tag + "_u", base_u)] + fg,
                         sig=tag + "_sp")
                    gi2 += 1
            e_dtw = P.ev(tag + "_sp")
            for dtl in range(4):
                P.op("sync",
                     lambda e, dtl=dtl, c=c, dir_=dir_: e.dma_start(
                         out=xcr[:, dir_ * 4 + dtl, :],
                         in_=xc_sp.ap()[dir_, dtl, :, c * MCH:(c + 1) * MCH]),
                     waits=[P.ev("s1_sp"), (tag + "_u", base_u)] + fg + rg,
                     sig=tag + "_xc", dma=True)
            P.op("vector",
                 lambda e, dir_=dir_: e.tensor_tensor(
                     ub[:], dtc[:, 0:4, :], xcr[:, dir_ * 4:dir_ * 4 + 4, :], OP.mult),
                 waits=[e_dtw, P.ev(tag + "_xc")], sig=tag + "_u")
            e_u = P.ev(tag + "_u")
            base_sc = P.n(tag + "_sc")
            base_ub2 = P.n(tag + "_ub")
            base_ch = P.n(tag + "_ch")
            base_dA = P.n(tag + "_dA")
            for n in range(NST):
                pb = n % 2
                P.op("sync",
                     lambda e, n=n, pb=pb, c=c, dir_=dir_: e.dma_start(
                         out=bexp[:, pb, :],
                         in_=bc_rows.ap()[dir_, n:n + 1, c * MCH:(c + 1) * MCH]
                         .partition_broadcast(128)),
                     waits=[e_bc, (tag + "_ub", max(0, base_ub2 + 4 * (n - 1)))]
                     + (fg if n < 2 else []),
                     sig=f"s3x_b{pb}", dma=True)
                P.op("sync",
                     lambda e, n=n, pb=pb, c=c, dir_=dir_: e.dma_start(
                         out=cexp[:, pb, :],
                         in_=bc_rows.ap()[dir_, NST + n:NST + n + 1, c * MCH:(c + 1) * MCH]
                         .partition_broadcast(128)),
                     waits=[e_bc, (tag + "_ch", max(0, base_ch + 4 * (n - 1)))]
                     + (fg if n < 2 else []),
                     sig=f"s3x_c{pb}", dma=True)
                e_bx = P.ev(f"s3x_b{pb}")
                e_cx = P.ev(f"s3x_c{pb}")
                for dtl in range(4):
                    i = n * 4 + dtl
                    pd = i % 2
                    P.op("scalar",
                         lambda e, dtl=dtl, n=n, pd=pd, dir_=dir_: e.activation(
                             dAb[:, pd, :], dtc[:, dtl, :], AF.Exp,
                             scale=A_f[:, dir_ * 4 + dtl, n:n + 1]),
                         waits=[e_A, (tag + "_sc", max(0, base_sc + i - 1))]
                         + (fg if i < 2 else []),
                         sig=tag + "_dA")
                    P.op("vector",
                         lambda e, dtl=dtl, pb=pb, pd=pd: e.tensor_tensor(
                             ubn[:, pd, :], ub[:, dtl, :], bexp[:, pb, :], OP.mult),
                         waits=[e_bx], sig=tag + "_ub")
                    col = dir_ * 128 + n * 8 + dtl

                    def sc(e, pd=pd, col=col, rev=rev):
                        d0 = dAb[:, pd, ::-1] if rev else dAb[:, pd, :]
                        d1 = ubn[:, pd, ::-1] if rev else ubn[:, pd, :]
                        ho = hb[:, pd, ::-1] if rev else hb[:, pd, :]
                        e.tensor_tensor_scan(ho, d0, d1, state[:, col:col + 1],
                                             OP.mult, OP.add)
                        src = hb[:, pd, 0:1] if rev else hb[:, pd, MCH - 1:MCH]
                        return e.tensor_copy(state[:, col:col + 1], src)
                    P.op("vector", sc,
                         waits=[e_st0, (tag + "_dA", base_dA + i + 1)],
                         sig=tag + "_sc")

                    def chacc(e, dtl=dtl, pb=pb, pd=pd, n=n):
                        if n == 0:
                            return e.tensor_tensor(ycur[:, dtl, :], hb[:, pd, :],
                                                   cexp[:, pb, :], OP.mult)
                        e.tensor_tensor(chb[:, pd, :], hb[:, pd, :],
                                        cexp[:, pb, :], OP.mult)
                        return e.tensor_add(ycur[:, dtl, :], ycur[:, dtl, :],
                                            chb[:, pd, :])
                    P.op("vector", chacc, waits=[e_cx], sig=tag + "_ch")
            yield ci, c

    for ci, c in scan_chunks(0, False, "s3a", None):
        e_y = P.ev("s3a_ch")
        for dtl in range(4):
            P.op("gpsimd",
                 lambda e, dtl=dtl, c=c: e.dma_start(
                     out=yf_sp.ap()[dtl, :, c * MCH:(c + 1) * MCH],
                     in_=ycur[:, dtl, :]),
                 waits=[e_y], sig="s3a_ysp", dma=True)
        P.op("vector", lambda e: e.engine_nop(), waits=[P.ev("s3a_ysp")])

    if maxstage == "s3a":
        return finish_any([P.ev("s3a_ysp"), P.ev("s3a_sc")], dtc[:, 0, :], [es_m3, es_m])

    e_s3a_done = [P.ev("s3a_u"), P.ev("s3a_ub"), P.ev("s3a_ch"),
                  P.ev("s3a_sc"), P.ev("s3a_dA"), P.ev("s3a_xc")]
    for ci, c in scan_chunks(1, maxstage != "s3bf", "s3b", P.ev("s3a_sp"),
                             first_guards=e_s3a_done, reader_tag="s3b_g"):
        e_y = P.ev("s3b_ch")
        base_g = P.n("s3b_g")
        fg2 = e_s3a_done if ci == 0 else []
        for dtl in range(4):
            P.op("sync",
                 lambda e, dtl=dtl, c=c: e.dma_start(
                     out=yfr[:, dtl, :], in_=yf_sp.ap()[dtl, :, c * MCH:(c + 1) * MCH]),
                 waits=[P.ev("s3a_ysp"), ("s3b_g", base_g)] + fg2,
                 sig="s3b_ld", dma=True)
            P.op("sync",
                 lambda e, dtl=dtl, c=c: e.dma_start(
                     out=xcr[:, dtl, :], in_=xc_sp.ap()[0, dtl, :, c * MCH:(c + 1) * MCH]),
                 waits=[("s3b_g", base_g)] + fg2, sig="s3b_ld", dma=True)
            for dir_ in range(2):
                P.op("sync",
                     lambda e, dtl=dtl, c=c, dir_=dir_: e.dma_start(
                         out=szr[:, dir_ * 4 + dtl, :],
                         in_=sz_sp.ap()[dir_, dtl, :, c * MCH:(c + 1) * MCH]),
                     waits=[P.ev("s1_zsp"), ("s3b_g", base_g)],
                     sig="s3b_ld", dma=True)
        e_ld = P.ev("s3b_ld")
        for dir_ in range(2):
            for dtl in range(4):
                def g1(e, dir_=dir_, dtl=dtl):
                    ysrc = yfr[:, dtl, :] if dir_ == 0 else ycur[:, dtl, :]
                    xsrc = xcr[:, dtl, :] if dir_ == 0 else xcr[:, 4 + dtl, :]
                    e.scalar_tensor_tensor(ygate[:, dir_ * 4 + dtl, :], xsrc,
                                           Dp[:, dir_ * 4 + dtl:dir_ * 4 + dtl + 1],
                                           ysrc, OP.mult, OP.add)
                    return e.tensor_tensor(ygate[:, dir_ * 4 + dtl, :],
                                           ygate[:, dir_ * 4 + dtl, :],
                                           szr[:, dir_ * 4 + dtl, :], OP.mult)
                P.op("vector", g1, waits=[e_ld, e_y], sig="s3b_g")
        e_g = P.ev("s3b_g")
        base_oe = P.n("s3b_oe")
        gidx3 = 0
        for ti in range(8):
            for nchk in range(2):
                ps = SLOT6[gidx3 % 4]

                def mm(e, ti=ti, nchk=nchk, ps=ps):
                    for g in range(8):
                        ins = e.matmul(ps, ygate[:, g, ti * 128:(ti + 1) * 128],
                                       weff[:, g, nchk * 512:(nchk + 1) * 512],
                                       start=(g == 0), stop=(g == 7))
                    return ins
                w = [e_g, e_weff]
                if gidx3 >= 4:
                    w.append(("s3b_oe", base_oe + gidx3 - 3))
                else:
                    w.append(("s3b_oe", base_oe))
                P.op("tensor", mm, waits=w, sig="s3b_om")
                podt = f"s3b_od{ti % 2}"
                P.op("vector",
                     lambda e, ti=ti, nchk=nchk, ps=ps: e.tensor_copy(
                         outp[:, ti % 2, nchk * 512:(nchk + 1) * 512], ps),
                     waits=[P.ev("s3b_om"), (podt, 16 * max(0, P.n(podt) // 16))],
                     sig="s3b_oe")
                gidx3 += 1
            P.op("gpsimd",
                 lambda e, ti=ti, c=c: e.dma_start(
                     out=mout_in.ap()[c * MCH + ti * 128: c * MCH + (ti + 1) * 128, :],
                     in_=outp[:, ti % 2, :]),
                 waits=[P.ev("s3b_oe")], sig=f"s3b_od{ti % 2}", dma=True)

    if maxstage in ("s3b", "s3bf"):
        return finish_any([P.ev("s3b_od0"), P.ev("s3b_od1")], outp[:, 0, :],
                          [es_m3, es_m])

    P.op("gpsimd",
         lambda e: e.collective_compute(
             "ReduceScatter", OP.add, replica_groups=GROUPS,
             ins=[mout_in.ap().opt()], outs=[mout_rs.ap().opt()]),
         waits=[P.ev("s3b_od0"), P.ev("s3b_od1")], sig="s4_cc")
    e_cc2 = P.ev("s4_cc")

    es_m3.close()
    es_m.close()
    P.fence([e_cc2])
    P.release(["s1_", "s2_", "s3_", "s3a_", "s3b_"])
    stream = sb("stream", [128, NTT, D])
    centbf = sb("centbf", [128, 8, NE], BF16)
    ebrow = sb("ebrow", [1, NE], BF16)
    gsc = sb("gsc", [128, 6])
    g_sb = sb("g_sb", [128, NTT, NE])
    scs = sb("scs", [128, NE])

    with nc.sbuf_tensor("projbb", [128, D], F32) as projbb, \
            nc.sbuf_tensor("rsstg", [128, 2, D], F32) as rsstg, \
            nc.sbuf_tensor("xstg2", [128, 2, D], F32) as xstg2:
        dmaop("sync", projbb[:], proj_b.partition_broadcast(128), "mw_ld")
        e_pb2 = P.ev("mw_ld")
        for ti in range(NTT):
            w = [e_cc2]
            if ti >= 2:
                w.append(P.w("s4_add", ti - 1))
            dmaop("sync", rsstg[:, ti % 2, :],
                  mout_rs.ap()[ti * 128:(ti + 1) * 128, :], f"s4_ld{ti % 2}", waits=w)
            dmaop("sync", xstg2[:, ti % 2, :],
                  x_tr[ti * 128:(ti + 1) * 128, :], f"s4_ld{ti % 2}", waits=w)

            def add2(e, ti=ti, projbb=projbb, rsstg=rsstg, xstg2=xstg2):
                e.tensor_add(stream[:, ti, :], xstg2[:, ti % 2, :], rsstg[:, ti % 2, :])
                return e.tensor_add(stream[:, ti, :], stream[:, ti, :], projbb[:])
            P.op("vector", add2,
                 waits=[(f"s4_ld{ti % 2}", 32 * (ti // 2 + 1)), e_pb2], sig="s4_add")
    e_x2 = P.ev("s4_add")
    if dbg:
        for ti in range(NTT):
            dmaop("gpsimd", dbg_t[0][ti * 128:(ti + 1) * 128, :], stream[:, ti, :],
                  "dbg_d", waits=[e_x2])

    def finish(e_last):
        for ti in range(NTT):
            dmaop("sync", out_t[ti * 128:(ti + 1) * 128, :], stream[:, ti, :],
                  "out_d", waits=[e_last])
        fin = [P.ev("out_d")] + ([P.ev("dbg_d")] if dbg else [])
        P.op("vector", lambda e: e.engine_nop(), waits=fin)
        P.emit_all(block)
        es.close()
        return nc
    if maxstage == 1:
        return finish(e_x2)

    with nc.sbuf_tensor("wsstg", [128, 8, D], F32) as wsstg, \
            nc.sbuf_tensor("wsstg2", [128, 8, D], F32) as wsstg2, \
            nc.sbuf_tensor("wshp", [128, 8, D], BF16) as wshp, \
            nc.sbuf_tensor("bstgB", [1, D], F32) as bstgB, \
            nc.sbuf_tensor("bstg", [2, D], F32) as bstg, \
            nc.sbuf_tensor("brstg", [NE, D], F32) as brstg, \
            nc.sbuf_tensor("brbf", [NE + 1, D], BF16) as brbf, \
            nc.sbuf_tensor("censtg", [128, 8, NE], F32) as censtg, \
            nc.sbuf_tensor("ebstg", [1, NE], F32) as ebstg:
        wsr = WsT_i.rearrange("n (kt p) m -> p (n kt) m", p=128)
        dmaop("sync", wsstg[:], wsr[:, 0:8, :], "mw_ld", waits=[e_x2])
        dmaop("sync", wsstg2[:], wsr[:, 8:16, :], "mw_ld", waits=[e_x2])
        dmaop("sync", bstg[0:1, :], bs_i[0:1, :], "mw_ld", waits=[e_x2])
        dmaop("sync", bstgB[0:1, :], bs_i[1:2, :], "mw_ld", waits=[e_x2])
        dmaop("sync", brstg[:], br_i, "mw_ld", waits=[e_x2])
        dmaop("sync", censtg[:], centT_i.rearrange("(kt p) n -> p kt n", p=128), "mw_ld", waits=[e_x2])
        dmaop("sync", ebstg[:], eb_i, "mw_ld", waits=[e_x2])

        def wsum(e, wsstg=wsstg, wsstg2=wsstg2, wshp=wshp, bstg=bstg,
                 bstgB=bstgB, brstg=brstg, censtg=censtg, ebstg=ebstg):
            e.tensor_add(wsstg[:], wsstg[:], wsstg2[:])
            e.tensor_scalar_mul(wsstg[:], wsstg[:], 0.5)
            e.tensor_copy(wshp[:], wsstg[:])
            e.tensor_copy(brbf[0:NE, :], brstg[:])
            e.tensor_add(bstg[0:1, :], bstg[0:1, :], bstgB[0:1, :])
            e.tensor_scalar_mul(bstg[0:1, :], bstg[0:1, :], 0.5)
            e.tensor_copy(brbf[NE:NE + 1, :].rearrange("p m -> p m"), bstg[0:1, :]) if False else None
            e.tensor_copy(bstgB[0:1, :], bstg[0:1, :])
            e.tensor_copy(centbf[:], censtg[:])
            return e.tensor_copy(ebrow[:], ebstg[:])
        P.op("vector", wsum, waits=[P.ev("mw_ld")], sig="mw_cv")
        P.op("gpsimd", lambda e, wshp=wshp: e.dma_start(out=wsh_sp.ap(), in_=wshp[:]),
             waits=[P.ev("mw_cv")], sig="mw_sp", dma=True)
        P.op("vector", lambda e, bstgB=bstgB, brbf=brbf: e.tensor_copy(
                 brbf[NE - 16:NE - 15, :], bstgB[0:1, :]) if False else
             e.tensor_copy(brbf[0:1, 0:1], brbf[0:1, 0:1]), waits=[], sig=None) if False else None
        P.op("vector", lambda e, bstgB=bstgB, brbf=brbf: e.tensor_copy(
                 brbf[NE:NE + 1, :].rearrange("p m -> p m"), bstgB[0:1, :]) if False
             else e.engine_nop(), waits=[], sig=None) if False else None
        P.op("gpsimd", lambda e, brbf=brbf: e.dma_start(
                 out=br_hb.ap()[0:NE, :], in_=brbf[0:NE, :]),
             waits=[P.ev("mw_cv")], sig="mw_sp", dma=True)
        P.op("vector", lambda e, bstgB=bstgB, brbf=brbf: e.tensor_copy(
                 brbf[0:1, :], bstgB[0:1, :]),
             waits=[P.ev("mw_sp")], sig="mw_cv")
        P.op("gpsimd", lambda e, brbf=brbf: e.dma_start(
                 out=br_hb.ap()[NE:NE + 1, :], in_=brbf[0:1, :]),
             waits=[P.ev("mw_cv")], sig="mw_sp", dma=True)
    e_mw = P.ev("mw_cv")
    e_mws = P.ev("mw_sp")

    def moe(call, e_start):
        tg = f"m{call}"
        es_moe = ExitStack()
        es_ln = ExitStack()
        xhatT = sb("xhatT" + tg, [128, 8, TR], BF16, stack=es_moe)
        wsh_bf = sb("wsh_bf" + tg, [128, 8, D], BF16, stack=es_moe)
        brp = sb("brp" + tg, [1, 2, D], BF16, stack=es_moe)
        xtok = sb("xtok" + tg, [128, NTT, D], BF16, stack=es_ln)
        xmf = sb("xmf" + tg, [128, 2, D], stack=es_ln)
        dmaop("sync", wsh_bf[:], wsh_sp.ap(), "me_shl", waits=[e_mws, e_start])
        e_shl = P.ev("me_shl")
        for ti in range(NTT):
            dw = [e_start] if ti < 2 else [P.w(tg + "_tp", 8 * (ti - 1))]
            lev = ln_tile(tg + "_ln", stream[:, ti, :], xmf[:, ti % 2, :],
                          dst_waits=dw)
            P.op("vector",
                 lambda e, ti=ti: e.tensor_copy(xtok[:, ti, :], xmf[:, ti % 2, :]),
                 waits=[lev], sig=tg + "_xtk")
            for j in range(8):
                extra = [lev, e_start]
                if ti > 0:
                    extra.append(P.w(tg + "_tpe", 8 * ti))
                P.op("tensor",
                     lambda e, j=j, ti=ti: e.transpose(
                         TPS[j], xmf[:, ti % 2, j * 128:(j + 1) * 128], id_bf[:]),
                     waits=extra, sig=tg + "_tp")
                P.op("vector",
                     lambda e, j=j, ti=ti: e.tensor_copy(
                         xhatT[:, j, ti * 128:(ti + 1) * 128], TPS[j]),
                     waits=[P.ev(tg + "_tp")], sig=tg + "_tpe")
        e_tp2 = P.ev(tg + "_tpe")
        for ti in range(NTT):
            def mmsc(e, ti=ti):
                for kt in range(8):
                    e.matmul(PSD[:, 0:NE], xhatT[:, kt, ti * 128:(ti + 1) * 128],
                             centbf[:, kt, :], start=(kt == 0), stop=False)
                return e.matmul(PSD[:, 0:NE], ones_bf[0:1, :], ebrow[:],
                                start=False, stop=True)
            P.op("tensor", mmsc, waits=[e_tp2, e_mw, P.ev(tg + "_gv")], sig=tg + "_gm")

            def gates(e, ti=ti):
                e.reduce_max(gsc[:, 0:1], PSD[:, 0:NE], axis=AXX, negate=True)
                return e.tensor_copy(scs[:], PSD[:, 0:NE])
            P.op("vector", gates, waits=[P.ev(tg + "_gm")], sig=tg + "_gv")
            P.op("scalar",
                 lambda e: e.activation(scs[:], scs[:], AF.Exp, bias=gsc[:, 0:1],
                                        accum_out=gsc[:, 1:2]),
                 waits=[P.ev(tg + "_gv")], sig=tg + "_gs")

            def top2(e, ti=ti):
                e.reciprocal(gsc[:, 2:3], gsc[:, 1:2])
                e.tensor_scalar_mul(scs[:], scs[:], gsc[:, 2:3])
                e.reduce_max(gsc[:, 3:4], scs[:], axis=AXX)
                e.tensor_scalar_mul(gsc[:, 4:5], gsc[:, 3:4], -1e30)
                e.tensor_single_scalar(g_sb[:, ti, :], scs[:], gsc[:, 3:4], OP.is_ge)
                e.scalar_tensor_tensor(g_sb[:, ti, :], g_sb[:, ti, :], gsc[:, 4:5],
                                       scs[:], OP.mult, OP.add)
                e.reduce_max(gsc[:, 5:6], g_sb[:, ti, :], axis=AXX)
                e.tensor_single_scalar(g_sb[:, ti, :], scs[:], gsc[:, 5:6], OP.is_ge)
                return e.tensor_tensor(g_sb[:, ti, :], g_sb[:, ti, :], scs[:], OP.mult)
            P.op("vector", top2, waits=[P.ev(tg + "_gs")], sig=tg + "_gv")
        e_gates = P.ev(tg + "_gv")
        for ti in range(NTT):
            P.op("vector",
                 lambda e, ti=ti: e.tensor_add(stream[:, ti, :], stream[:, ti, :],
                                               xtok[:, ti, :]),
                 sig=tg + "_ai")
        e_ai = P.ev(tg + "_ai")
        es_ln.close()
        with nc.sbuf_tensor(f"wstg{call}", [128, 2, 4, D], F32) as wstg, \
                nc.sbuf_tensor(f"wbf{call}", [128, 2, 8, D], BF16) as wbf:
            jglob = 0
            for e_i in range(NE + 1):
                pw = e_i % 2
                if e_i < NE:
                    if call == 0:
                        for half in range(2):
                            ws = (2 * e_i + half) % 2
                            dmaop("sync", wstg[:, ws, 0:4, :],
                                  WrT_i.rearrange("n (kt p) m -> n p kt m", p=128)
                                  [e_i, :, half * 4:(half + 1) * 4, :],
                                  "a_wl",
                                  waits=[(tg + "_wc", max(0, P.n(tg + "_wc") - 1)),
                                         e_start, e_ai])
                            P.op("vector",
                                 lambda e, pw=pw, half=half, ws=ws: e.tensor_copy(
                                     wbf[:, pw, half * 4:(half + 1) * 4, :],
                                     wstg[:, ws, 0:4, :]),
                                 waits=[P.ev("a_wl"),
                                        (tg + f"_ws{pw}", 16 * (e_i // 2)),
                                        P.w(tg + "_em", max(0, jglob - 16))],
                                 sig=tg + "_wc")
                        P.op("gpsimd",
                             lambda e, e_i=e_i, pw=pw: e.dma_start(
                                 out=wr_bf_sp.ap()[e_i], in_=wbf[:, pw, :, :]),
                             waits=[P.ev(tg + "_wc")], sig=tg + f"_ws{pw}",
                             dma=True)
                    else:
                        dmaop("sync", wbf[:, pw, :, :], wr_bf_sp.ap()[e_i],
                              tg + f"_wd{pw}",
                              waits=[("m0_ws0", 16 * (NE // 2)),
                                     ("m0_ws1", 16 * (NE // 2)), e_start, e_ai,
                                     P.w(tg + "_em", max(0, jglob - 16))])
                if e_i >= NE:
                    e_w = e_shl
                elif call == 0:
                    e_w = P.ev(tg + "_wc")
                else:
                    e_w = P.ev(tg + f"_wd{pw}")
                P.op("sync",
                     lambda e, e_i=e_i, pw=pw: e.dma_start(
                         out=brp[0:1, pw, :], in_=br_hb.ap()[e_i:e_i + 1, :]),
                     waits=[e_mws, P.w(tg + "_em", max(0, jglob - 16))],
                     sig=f"me_brl{pw}", dma=True)
                e_br = P.ev(f"me_brl{pw}")
                brow = brp[0:1, pw, :]
                for ti in range(NTT):
                    psfull = [PSA, PSB][jglob % 2]

                    def mme(e, ti=ti, e_i=e_i, pw=pw, psfull=psfull, brow=brow):
                        ins = None
                        for nchk in range(2):
                            dstp = psfull[:, nchk * 512:(nchk + 1) * 512]
                            for kt in range(8):
                                if e_i < NE:
                                    rhs = wbf[:, pw, kt, nchk * 512:(nchk + 1) * 512]
                                else:
                                    rhs = wsh_bf[:, kt, nchk * 512:(nchk + 1) * 512]
                                e.matmul(dstp, xhatT[:, kt, ti * 128:(ti + 1) * 128],
                                         rhs, start=(kt == 0), stop=False)
                            ins = e.matmul(dstp, ones_bf[0:1, :],
                                           brow[:, nchk * 512:(nchk + 1) * 512],
                                           start=False, stop=True)
                        return ins
                    w = [e_w, e_gates, e_ai, e_br]
                    if jglob >= 2:
                        w.append(P.w(tg + "_ee", jglob - 1))
                    P.op("tensor", mme, waits=w, sig=tg + "_em")

                    def evx(e, ti=ti, e_i=e_i, psfull=psfull):
                        if e_i < NE:
                            return e.scalar_tensor_tensor(
                                stream[:, ti, :], psfull[:], g_sb[:, ti, e_i:e_i + 1],
                                stream[:, ti, :], OP.mult, OP.add)
                        return e.tensor_add(stream[:, ti, :], stream[:, ti, :],
                                            psfull[:])
                    P.op("vector", evx, waits=[P.ev(tg + "_em")], sig=tg + "_ee")
                    jglob += 1
        es_moe.close()
        return P.ev(tg + "_ee")

    e_x3 = moe(0, e_x2)
    if dbg:
        for ti in range(NTT):
            dmaop("gpsimd", dbg_t[1][ti * 128:(ti + 1) * 128, :], stream[:, ti, :],
                  "dbg_d", waits=[e_x3])
    if maxstage == 2:
        return finish(e_x3)

    es_a = ExitStack()
    es_a1 = ExitStack()
    P.fence([e_x3, P.ev("m0_ws0"), P.ev("m0_ws1")])
    P.release(["s4_", "mw_", "m0_"])
    wupk_bf = sb("wupk_bf", [128, 2, D], BF16, stack=es_a)
    wupv_bf = sb("wupv_bf", [128, 2, D], BF16, stack=es_a)
    latT = sb("latT", [128, 2, TR], BF16, stack=es_a)
    nrow = sb("nrow", [1, 8], F32, stack=es_a)
    mneg = sb("mneg", [128, 1], F32, stack=es_a)
    attr = sb("attr", [128, 2, 8, 128], BF16, stack=es_a)
    bqrow = sb("bqrow", [1, D], BF16, stack=es_a)
    bkrow = sb("bkrow", [1, D], BF16, stack=es_a)
    bvrow = sb("bvrow", [1, D], BF16, stack=es_a)
    bdrow = sb("bdrow", [1, DLAT], BF16, stack=es_a)
    borow = sb("borow", [1, D], BF16, stack=es_a)
    qstg = sb("qstg", [128, 2048], F32, stack=es_a1)
    tmpbf = sb("tmpbf", [128, 2048], BF16, stack=es_a1)
    wq_bf = sb("wq_bf", [128, 8, D], BF16, stack=es_a1)
    wdn_bf = sb("wdn_bf", [128, 8, DLAT], BF16, stack=es_a1)
    xhatTa = sb("xhatTa", [128, 8, TR], BF16, stack=es_a1)
    xh2 = sb("xh2", [128, 2, D], stack=es_a1)
    qkb = xh2
    ropb = sb("ropb", [128, 2, D], stack=es_a1)
    rsc2 = sb("rsc2", [128, 2, 512], stack=es_a1)
    lstg = sb("lstg", [128, 2, DLAT], F32, stack=es_a1)

    def loadc(dst, src, ngrp, width):
        v = qstg[:, 0:ngrp * width].rearrange("p (g w) -> p g w", g=ngrp)
        dmaop("sync", v, src, "a_wl", waits=[P.ev("a_wc")])
        P.op("vector", lambda e, d=dst, v=v: e.tensor_copy(d, v),
             waits=[P.ev("a_wl")], sig="a_wc")

    wqr = Wq_i.rearrange("(kt p) m -> p kt m", p=128)
    for qq in range(4):
        loadc(wq_bf[:, 2 * qq:2 * qq + 2, :], wqr[:, 2 * qq:2 * qq + 2, :], 2, D)
    for qq in range(4):
        loadc(wdn_bf[:, 2 * qq:2 * qq + 2, :], Wdown_i.rearrange(
            "(kt p) m -> p kt m", p=128)[:, 2 * qq:2 * qq + 2, :], 2, DLAT)
    loadc(wupk_bf[:], Wup_k_i.rearrange("(kt p) m -> p kt m", p=128), 2, D)
    loadc(wupv_bf[:], Wup_v_i.rearrange("(kt p) m -> p kt m", p=128), 2, D)
    wor = Wout_i.rearrange("(kt p) m -> p kt m", p=128)
    for qq in range(4):
        dmaop("sync", qstg[:].rearrange("p (g m) -> p g m", g=2),
              wor[:, 2 * qq:2 * qq + 2, :], "a_wl",
              waits=[P.ev("a_wc"), P.ev("a_wos")])
        P.op("vector", lambda e: e.tensor_copy(tmpbf[:], qstg[:]),
             waits=[P.ev("a_wl")], sig="a_wc")
        P.op("gpsimd",
             lambda e, qq=qq: e.dma_start(
                 out=wo_sp.ap()[:, 2 * qq:2 * qq + 2, :],
                 in_=tmpbf[:].rearrange("p (g m) -> p g m", g=2)),
             waits=[P.ev("a_wc")], sig="a_wos", dma=True)
    with nc.sbuf_tensor("bstg2", [1, D], F32) as bstg2:
        for dst, src, wd in ((bqrow, bq_i, D), (bkrow, bup_k_i, D),
                             (bvrow, bup_v_i, D), (bdrow, bdown_i, DLAT),
                             (borow, bout_i, D)):
            dmaop("sync", bstg2[0:1, 0:wd], src, "a_bl2", waits=[P.ev("a_bc")])
            P.op("vector",
                 lambda e, b=bstg2, d=dst, wd=wd: e.tensor_copy(d[:], b[0:1, 0:wd]),
                 waits=[P.ev("a_bl2")], sig="a_bc")
    e_aw = [P.ev("a_wc"), P.ev("a_bc")]
    if maxstage == "aw":
        return finish_any(e_aw + [P.ev("a_wos")], qstg, [es_a1, es_a, es_m])

    for ti in range(NTT):
        dw = [] if ti < 2 else [P.w("a_tp", 8 * (ti - 1))]
        lev = ln_tile("a_ln", stream[:, ti, :], xh2[:, ti % 2, :], dst_waits=dw)
        for j in range(8):
            extra = [lev]
            if ti > 0:
                extra.append(P.w("a_tpe", 8 * ti))
            P.op("tensor",
                 lambda e, j=j, ti=ti: e.transpose(
                     TPS[j], xh2[:, ti % 2, j * 128:(j + 1) * 128], id_bf[:]),
                 waits=extra, sig="a_tp")
            P.op("vector",
                 lambda e, j=j, ti=ti: e.tensor_copy(
                     xhatTa[:, j, ti * 128:(ti + 1) * 128], TPS[j]),
                 waits=[P.ev("a_tp")], sig="a_tpe")
    e_tp3 = P.ev("a_tpe")

    def rope_tok(e, buf, tbl, scr):
        x1 = buf.rearrange("p (h m) -> p h m", h=16)[:, :, 0:32]
        x2 = buf.rearrange("p (h m) -> p h m", h=16)[:, :, 32:64]
        cosv = tbl[:, 0:512].rearrange("p (h m) -> p h m", h=16)
        sinv = tbl[:, 512:1024].rearrange("p (h m) -> p h m", h=16)
        s0 = scr[:, 0, :].rearrange("p (h m) -> p h m", h=16)
        s1 = scr[:, 1, :].rearrange("p (h m) -> p h m", h=16)
        e.tensor_tensor(s0, x1, cosv, OP.mult)
        e.tensor_tensor(s1, x2, sinv, OP.mult)
        e.tensor_sub(s0, s0, s1)
        e.tensor_tensor(s1, x1, sinv, OP.mult)
        e.tensor_copy(x1, s0)
        e.tensor_tensor(s0, x2, cosv, OP.mult)
        return e.tensor_add(x2, s1, s0)

    qmx = sb("qmx", [128, 4], F32, stack=es_a1)
    nsq = sb("nsq", [128, 16], F32, stack=es_a1)
    rsf = rsc2[:, :, :].rearrange("p a b -> p (a b)")
    P.op("vector", lambda e: e.memset(qmx[:], 0.0), sig="a_nv")
    for ti in range(NTT):
        dmaop("sync", ropb[:, ti % 2, :],
              ropeq_tok[ti * 128:(ti + 1) * 128, :], f"a_rl{ti % 2}",
              waits=[("a_qr", max(0, P.n("a_qr") - 1))])

        def mmq(e, ti=ti):
            ins = None
            for chk in range(2):
                dstp = [PSD, PSE][chk]
                for kt in range(8):
                    e.matmul(dstp[:], xhatTa[:, kt, ti * 128:(ti + 1) * 128],
                             wq_bf[:, kt, chk * 512:(chk + 1) * 512],
                             start=(kt == 0), stop=False)
                ins = e.matmul(dstp[:], ones_bf[0:1, :],
                               bqrow[:, chk * 512:(chk + 1) * 512],
                               start=False, stop=True)
            return ins
        w = e_aw + [e_tp3]
        if ti > 0:
            w.append(("a_qe", P.n("a_qe")))
        P.op("tensor", mmq, waits=w, sig="a_qm")
        P.op("vector",
             lambda e, ti=ti: (e.tensor_copy(qkb[:, ti % 2, 0:512], PSD[:]),
                               e.tensor_copy(qkb[:, ti % 2, 512:1024], PSE[:]))[-1],
             waits=[P.ev("a_qm"), (f"a_qsp{ti % 2}", 16 * (ti // 2))] +
             ([P.ev("a_wos")] if ti == 0 else []),
             sig="a_qe")
        P.op("vector",
             lambda e, ti=ti: rope_tok(e, qkb[:, ti % 2, :], ropb[:, ti % 2, :], rsc2),
             waits=[(f"a_rl{ti % 2}", 16 * (ti // 2 + 1))], sig="a_qr")

        def nrm(e, ti=ti):
            e.tensor_tensor(rsf[:], qkb[:, ti % 2, :], qkb[:, ti % 2, :], OP.mult)
            for h in range(16):
                e.reduce_sum(nsq[:, h:h + 1], rsf[:, h * 64:(h + 1) * 64], axis=AXX)
            e.reduce_max(qmx[:, 2:3], nsq[:], axis=AXX)
            return e.tensor_max(qmx[:, 0:1], qmx[:, 0:1], qmx[:, 2:3])
        P.op("vector", nrm, waits=[P.ev("a_qr")], sig="a_nv")
        P.op("gpsimd",
             lambda e, ti=ti: e.dma_start(
                 out=q_sp.ap()[ti * 128:(ti + 1) * 128, :], in_=qkb[:, ti % 2, :]),
             waits=[P.ev("a_nv")], sig=f"a_qsp{ti % 2}", dma=True)
    if maxstage == "aq":
        return finish_any([P.ev("a_qsp0"), P.ev("a_qsp1"), P.ev("a_qe")], qstg,
                          [es_a1, es_a, es_m])

    for ti in range(NTT):
        def mml(e, ti=ti):
            for kt in range(8):
                e.matmul(PSD[:, 0:DLAT], xhatTa[:, kt, ti * 128:(ti + 1) * 128],
                         wdn_bf[:, kt, :], start=(kt == 0), stop=False)
            return e.matmul(PSD[:, 0:DLAT], ones_bf[0:1, :], bdrow[:],
                            start=False, stop=True)
        w = e_aw + [e_tp3, ("a_qm", P.n("a_qm")), ("a_qe", P.n("a_qe"))]
        if ti > 0:
            w.append(P.w("a_le", ti))
        P.op("tensor", mml, waits=w, sig="a_lm")
        P.op("vector",
             lambda e, ti=ti: e.tensor_copy(lstg[:, ti % 2, :], PSD[:, 0:DLAT]),
             waits=[P.ev("a_lm"), P.w("a_lt", 2 * max(0, ti - 1))], sig="a_le")
        for j in range(2):
            P.op("tensor",
                 lambda e, ti=ti, j=j: e.transpose(
                     TPS[j], lstg[:, ti % 2, j * 128:(j + 1) * 128], id_bf[:]),
                 waits=[P.w("a_le", ti + 1),
                        P.w("a_lte", max(0, 2 * (ti - 1) + j + 1) if ti > 0 else 0)],
                 sig="a_lt")
            P.op("vector",
                 lambda e, ti=ti, j=j: e.tensor_copy(
                     latT[:, j, ti * 128:(ti + 1) * 128], TPS[j]),
                 waits=[P.ev("a_lt")], sig="a_lte")

    for ti in range(NTT):
        dmaop("sync", ropb[:, ti % 2, :],
              ropeq_tok[ti * 128:(ti + 1) * 128, :], f"a_krl{ti % 2}",
              waits=[("a_kr2", max(0, P.n("a_kr2") - 1)),
                     ("a_qr", P.n("a_qr")), ("a_nv", P.n("a_nv"))])

        def mmk(e, ti=ti):
            ins = None
            for chk in range(2):
                dstp = [PSD, PSE][chk]
                for lt in range(2):
                    e.matmul(dstp[:], latT[:, lt, ti * 128:(ti + 1) * 128],
                             wupk_bf[:, lt, chk * 512:(chk + 1) * 512],
                             start=(lt == 0), stop=False)
                ins = e.matmul(dstp[:], ones_bf[0:1, :],
                               bkrow[:, chk * 512:(chk + 1) * 512],
                               start=False, stop=True)
            return ins
        w = e_aw + [P.w("a_lte", 2 * (ti + 1)), ("a_lm", P.n("a_lm"))]
        if ti > 0:
            w.append(("a_ke", P.n("a_ke")))
        else:
            w.append(("a_le", P.n("a_le")))
        P.op("tensor", mmk, waits=w, sig="a_km")
        P.op("vector",
             lambda e, ti=ti: (e.tensor_copy(qkb[:, ti % 2, 0:512], PSD[:]),
                               e.tensor_copy(qkb[:, ti % 2, 512:1024], PSE[:]))[-1],
             waits=[P.ev("a_km"), (f"a_ksp{ti % 2}", 16 * (ti // 2)),
                    ("a_qsp0", P.n("a_qsp0")), ("a_qsp1", P.n("a_qsp1"))],
             sig="a_ke")
        P.op("vector",
             lambda e, ti=ti: rope_tok(e, qkb[:, ti % 2, :], ropb[:, ti % 2, :], rsc2),
             waits=[(f"a_krl{ti % 2}", 16 * (ti // 2 + 1))], sig="a_kr2")

        def nrmk(e, ti=ti):
            e.tensor_tensor(rsf[:], qkb[:, ti % 2, :], qkb[:, ti % 2, :], OP.mult)
            for h in range(16):
                e.reduce_sum(nsq[:, h:h + 1], rsf[:, h * 64:(h + 1) * 64], axis=AXX)
            e.reduce_max(qmx[:, 2:3], nsq[:], axis=AXX)
            return e.tensor_max(qmx[:, 1:2], qmx[:, 1:2], qmx[:, 2:3])
        P.op("vector", nrmk, waits=[P.ev("a_kr2")], sig="a_nv")
        P.op("gpsimd",
             lambda e, ti=ti: e.dma_start(
                 out=k_sp.ap()[ti * 128:(ti + 1) * 128, :], in_=qkb[:, ti % 2, :]),
             waits=[P.ev("a_nv")], sig=f"a_ksp{ti % 2}", dma=True)

        def mmv(e, ti=ti):
            ins = None
            for chk in range(2):
                dstp = PSC[:, chk * 512:(chk + 1) * 512]
                for lt in range(2):
                    e.matmul(dstp, latT[:, lt, ti * 128:(ti + 1) * 128],
                             wupv_bf[:, lt, chk * 512:(chk + 1) * 512],
                             start=(lt == 0), stop=False)
                ins = e.matmul(dstp, ones_bf[0:1, :],
                               bvrow[:, chk * 512:(chk + 1) * 512],
                               start=False, stop=True)
            return ins
        w = [P.w("a_lte", 2 * (ti + 1))] + e_aw
        if ti > 0:
            w.append(("a_ve", P.n("a_ve")))
        P.op("tensor", mmv, waits=w, sig="a_vm")
        P.op("vector",
             lambda e, ti=ti: e.tensor_copy(
                 tmpbf[:, (ti % 2) * 1024:(ti % 2) * 1024 + 1024], PSC[:]),
             waits=[P.ev("a_vm"), (f"a_vsp{ti % 2}", 16 * (ti // 2))], sig="a_ve")
        P.op("gpsimd",
             lambda e, ti=ti: e.dma_start(
                 out=v_sp.ap()[ti * 128:(ti + 1) * 128, :],
                 in_=tmpbf[:, (ti % 2) * 1024:(ti % 2) * 1024 + 1024]),
             waits=[P.ev("a_ve")], sig=f"a_vsp{ti % 2}", dma=True)

    P.op("tensor",
         lambda e: (e.transpose(PSD[0:1, 0:128], qmx[:, 0:1], id_bf[:]),
                    e.transpose(PSE[0:1, 0:128], qmx[:, 1:2], id_bf[:]))[-1],
         waits=[P.ev("a_nv"), ("a_km", P.n("a_km")), ("a_ke", P.n("a_ke"))],
         sig="a_nm")

    def nred(e):
        e.reduce_max(nrow[0:1, 0:1], PSD[0:1, 0:128], axis=AXX)
        e.reduce_max(nrow[0:1, 1:2], PSE[0:1, 0:128], axis=AXX)
        return e.tensor_tensor(nrow[0:1, 6:7], nrow[0:1, 0:1], nrow[0:1, 1:2],
                               OP.mult)
    P.op("vector", nred, waits=[P.ev("a_nm")], sig="a_nr")

    def mbias(e):
        e.activation(nrow[0:1, 6:7], nrow[0:1, 6:7], AF.Ln, bias=epsc[0:1, 1:2])
        return e.activation(nrow[0:1, 7:8], nrow[0:1, 6:7], AF.Exp, scale=0.5)
    P.op("scalar", mbias, waits=[P.ev("a_nr"), e_const], sig="a_ns")
    P.op("vector",
         lambda e: e.tensor_scalar_mul(nrow[0:1, 7:8], nrow[0:1, 7:8], -0.125),
         waits=[P.ev("a_ns")], sig="a_nr")
    P.op("gpsimd", lambda e: e.dma_start(out=mb_hb.ap(), in_=nrow[0:1, 7:8]),
         waits=[P.ev("a_nr")], sig="a_rw", dma=True)
    dmaop("sync", mneg[:], mb_hb.ap().partition_broadcast(128), "a_nb",
          waits=[P.ev("a_rw")])
    e_mb = P.ev("a_nb")
    if maxstage == "ak":
        return finish_any([P.ev("a_ksp0"), P.ev("a_ksp1"), P.ev("a_vsp0"),
                           P.ev("a_vsp1"), e_mb], qstg, [es_a1, es_a, es_m])

    es_a1.close()
    P.fence([P.ev("a_qsp0"), P.ev("a_qsp1"), P.ev("a_ksp0"), P.ev("a_ksp1"),
             P.ev("a_vsp0"), P.ev("a_vsp1"), P.ev("a_wos"), e_mb])
    blkb = sb("blkb", [128, 2, 2, 64], F32, stack=es_a)
    vbb = sb("vbb", [128, 2, 64], BF16, stack=es_a)
    qkt = sb("qkt", [64, 2, 2, 128], BF16, stack=es_a)
    pb = sb("pb", [128, 2, 128], BF16, stack=es_a)
    bdm = sb("bdm", [128, 128], BF16, stack=es_a)
    rden = sb("rden", [128, 2], F32, stack=es_a)
    attb = sb("attb", [128, 2, 64], BF16, stack=es_a)
    P.op("gpsimd", lambda e: e.dma_start(out=bdm[:], in_=bdmask_i),
         sig="a_bm", dma=True)
    e_bdm = P.ev("a_bm")
    qf = q_sp.ap().rearrange("t (h d) -> (t h) d", d=64)
    kf = k_sp.ap().rearrange("t (h d) -> (t h) d", d=64)
    vf = v_sp.ap().rearrange("t (h d) -> (t h) d", d=64)
    af = att_sp.ap().rearrange("t (h d) -> (t h) d", d=64)
    NBLK = TR * 16 // 128
    for blk in range(NBLK):
        pi = blk % 2
        P.op("gpsimd",
             lambda e, blk=blk, pi=pi: e.dma_start(
                 out=blkb[:, pi, 0, :], in_=qf[blk * 128:(blk + 1) * 128, :]),
             waits=[P.w("a_btp", 2 * max(0, blk - 1))], sig=f"a_bl{pi}", dma=True)
        P.op("gpsimd",
             lambda e, blk=blk, pi=pi: e.dma_start(
                 out=blkb[:, pi, 1, :], in_=kf[blk * 128:(blk + 1) * 128, :]),
             waits=[P.w("a_btp", 2 * max(0, blk - 1))], sig=f"a_bl{pi}", dma=True)
        P.op("sync",
             lambda e, blk=blk, pi=pi: e.dma_start(
                 out=vbb[:, pi, :], in_=vf[blk * 128:(blk + 1) * 128, :]),
             waits=[P.w("a_bav", 2 * max(0, blk - 1))], sig=f"a_vl{pi}", dma=True)
        for j in range(2):
            P.op("tensor",
                 lambda e, j=j, pi=pi: e.transpose(
                     [PSA, PSB][pi][0:64, j * 512:j * 512 + 128],
                     blkb[:, pi, j, :], id_bf[:]),
                 waits=[(f"a_bl{pi}", 32 * (blk // 2 + 1)),
                        P.w("a_bte", 2 * max(0, blk - 1))],
                 sig="a_btp")
        P.op("vector",
             lambda e, pi=pi: (e.tensor_copy(qkt[:, 0, pi, :],
                                             [PSA, PSB][pi][0:64, 0:128]),
                               e.tensor_copy(qkt[:, 1, pi, :],
                                             [PSA, PSB][pi][0:64, 512:640]))[-1],
             waits=[P.ev("a_btp"), P.w("a_bsc", max(0, blk - 1))],
             sig=[("a_bte", 2)])
        psc_ = [PSC[:, 0:128], PSC[:, 512:640]][pi]
        P.op("tensor",
             lambda e, pi=pi, psc_=psc_: e.matmul(
                 psc_, qkt[:, 1, pi, :], qkt[:, 0, pi, :], start=True, stop=True),
             waits=[P.w("a_bte", 2 * (blk + 1)), P.w("a_bex", max(0, blk - 1))],
             sig="a_bsc")
        P.op("scalar",
             lambda e, pi=pi, psc_=psc_: e.activation(
                 pb[:, pi, :], psc_, AF.Exp, scale=0.125, bias=mneg[:, 0:1]),
             waits=[P.w("a_bsc", blk + 1), e_mb,
                    P.w("a_bav", 2 * max(0, blk - 1))],
             sig="a_bex")
        P.op("vector",
             lambda e, pi=pi: e.tensor_tensor(pb[:, pi, :], pb[:, pi, :], bdm[:],
                                              OP.mult),
             waits=[P.w("a_bex", blk + 1), e_bdm], sig="a_bmk")
        psav = [PSD, PSE][pi]
        P.op("tensor",
             lambda e, pi=pi, psav=psav: e.matmul(
                 psav[:, 0:64], pb[:, pi, :], vbb[:, pi, :], start=True, stop=True),
             waits=[P.w("a_bmk", blk + 1), (f"a_vl{pi}", 16 * (blk // 2 + 1)),
                    P.w("a_brc", max(0, blk - 1)), P.w("a_bev", max(0, blk - 1))],
             sig="a_bav")
        P.op("tensor",
             lambda e, pi=pi, psav=psav: e.matmul(
                 psav[:, 64:65], pb[:, pi, :], ones_bf[:, 0:1],
                 start=True, stop=True),
             sig="a_bav")
        P.op("vector",
             lambda e, pi=pi, psav=psav: e.reciprocal(rden[:, pi:pi + 1],
                                                      psav[:, 64:65]),
             waits=[P.w("a_bav", 2 * (blk + 1))], sig="a_brc")
        P.op("vector",
             lambda e, pi=pi, psav=psav: e.tensor_scalar_mul(
                 attb[:, pi, :], psav[:, 0:64], rden[:, pi:pi + 1]),
             waits=[P.w("a_brc", blk + 1), (f"a_bsp{pi}", 16 * (blk // 2))],
             sig="a_bev")
        P.op("gpsimd",
             lambda e, blk=blk, pi=pi: e.dma_start(
                 out=af[blk * 128:(blk + 1) * 128, :], in_=attb[:, pi, :]),
             waits=[P.w("a_bev", blk + 1)], sig=f"a_bsp{pi}", dma=True)
    if maxstage == "ab":
        return finish_any([P.ev("a_bsp0"), P.ev("a_bsp1")], bdm, [es_a, es_m])

    wo_sb = sb("wo_sb", [128, 8, D], BF16, stack=es_a)
    atok = sb("atok", [128, 2, D], F32, stack=es_a)
    dmaop("sync", wo_sb[:].rearrange("p g m -> p (g m)"), wo_sp.ap()
          .rearrange("p g m -> p (g m)"), "a_wor", waits=[P.ev("a_wos")])
    e_wor = P.ev("a_wor")
    tpso = [PSC[:, 0:128], PSC[:, 512:640], PSD[:, 0:128], PSE[:, 0:128]]
    for ti in range(NTT):
        P.op("gpsimd",
             lambda e, ti=ti: e.dma_start(
                 out=atok[:, ti % 2, :],
                 in_=att_sp.ap()[ti * 128:(ti + 1) * 128, :]),
             waits=[P.w("a_otp", 8 * max(0, ti - 1)),
                    ("a_bsp0", P.n("a_bsp0")), ("a_bsp1", P.n("a_bsp1"))],
             sig=f"a_or{ti % 2}", dma=True)
        for kt in range(8):
            w = [(f"a_or{ti % 2}", 16 * (ti // 2 + 1)),
                 P.w("a_ote", max(0, 8 * ti + kt - 3))]
            if ti == 0:
                w += [P.w("a_brc", NBLK), P.w("a_bev", NBLK),
                      P.w("a_bex", NBLK)]
            P.op("tensor",
                 lambda e, ti=ti, kt=kt: e.transpose(
                     tpso[kt % 4], atok[:, ti % 2, kt * 128:(kt + 1) * 128],
                     id_bf[:]),
                 waits=w, sig="a_otp")
            P.op("vector",
                 lambda e, ti=ti, kt=kt: e.tensor_copy(attr[:, ti % 2, kt, :],
                                                       tpso[kt % 4]),
                 waits=[P.ev("a_otp"), P.w("a_om", max(0, ti - 1))], sig="a_ote")

        def mmo(e, ti=ti):
            ins = None
            psf = [PSA, PSB][ti % 2]
            for chk in range(2):
                for kt in range(8):
                    e.matmul(psf[:, chk * 512:(chk + 1) * 512],
                             attr[:, ti % 2, kt, :],
                             wo_sb[:, kt, chk * 512:(chk + 1) * 512],
                             start=(kt == 0), stop=False)
                ins = e.matmul(psf[:, chk * 512:(chk + 1) * 512],
                               ones_bf[0:1, :], borow[:, chk * 512:(chk + 1) * 512],
                               start=False, stop=True)
            return ins
        w = [P.w("a_ote", 8 * (ti + 1)), e_wor]
        if ti == 0:
            w.append(P.w("a_bte", 2 * NBLK))
        if ti >= 2:
            w.append(P.w("a_oe", ti - 1))
        P.op("tensor", mmo, waits=w, sig="a_om")
        P.op("vector",
             lambda e, ti=ti: e.tensor_add(stream[:, ti, :], stream[:, ti, :],
                                           [PSA, PSB][ti % 2][:]),
             waits=[P.ev("a_om")], sig="a_oe")
    e_x4 = P.ev("a_oe")
    if dbg:
        for ti in range(NTT):
            dmaop("gpsimd", dbg_t[2][ti * 128:(ti + 1) * 128, :], stream[:, ti, :],
                  "dbg_d", waits=[e_x4])

    es_a.close()
    if maxstage == 3:
        return finish(e_x4)

    P.fence([e_x4])
    P.release(["a_"])
    e_x5 = moe(1, e_x4)
    if dbg:
        for ti in range(NTT):
            dmaop("gpsimd", dbg_t[3][ti * 128:(ti + 1) * 128, :], stream[:, ti, :],
                  "dbg_d", waits=[e_x5])
    for ti in range(NTT):
        dmaop("sync", out_t[ti * 128:(ti + 1) * 128, :], stream[:, ti, :],
              "out_d", waits=[e_x5])
    fin = [P.ev("out_d")] + ([P.ev("dbg_d")] if dbg else [])
    P.op("vector", lambda e: e.engine_nop(), waits=fin)

    P.emit_all(block)
    es.close()
    return nc


def kernel(**inputs):
    dbg = os.environ.get("KDBG", "") == "1"
    import ml_dtypes
    ii = {k: np.asarray(v) for k, v in inputs.items()}
    x = ii["L0_attn_a_x"]
    Wup = ii["L0_attn_b_Wup"].reshape(DLAT, H, 2, DH)
    bup = ii["L0_attn_b_bup"].reshape(H, 2, DH)

    t_all = np.arange(T, dtype=np.float64)
    inv = 1.0 / (10000.0 ** (np.arange(0, DH, 2, dtype=np.float64) / DH))
    frq = t_all[:, None] * inv[None, :]
    ropek = np.concatenate([np.tile(np.cos(frq), (1, 16)),
                            np.tile(np.sin(frq), (1, 16))], 1).astype(np.float32)
    bdmask = np.kron(np.eye(8), np.ones((16, 16))).astype(np.float32)

    in_maps = []
    for c in range(8):
        b, q = divmod(c, 4)
        sl = slice(q * DIL, (q + 1) * DIL)
        cols = np.r_[q * DIL:(q + 1) * DIL, 2048 + q * DIL:2048 + (q + 1) * DIL]
        m = {
            "x_b": np.ascontiguousarray(x[b]),
            "x_tr": np.ascontiguousarray(x[b, q * TR:(q + 1) * TR]),
            "in_w_l": np.ascontiguousarray(ii["L0_attn_a_in_w"][:, :, cols]),
            "conv_w_l": np.ascontiguousarray(ii["L0_attn_a_conv_w"][:, sl].reshape(2, 4, 128, DCONV).transpose(2, 0, 1, 3)),
            "conv_b_l": np.ascontiguousarray(ii["L0_attn_a_conv_b"][:, sl].reshape(2, 4, 128).transpose(2, 0, 1).reshape(128, 8)),
            "xp_w_l": np.ascontiguousarray(ii["L0_attn_a_xp_w"][:, sl]),
            "dt_w_l": np.ascontiguousarray(ii["L0_attn_a_dt_w"][:, :, sl]),
            "dt_b_l": np.ascontiguousarray(ii["L0_attn_a_dt_b"][:, sl].reshape(2, 4, 128).transpose(2, 0, 1).reshape(128, 8)),
            "A_log_l": np.ascontiguousarray(ii["L0_attn_a_A_log"][:, sl].reshape(2, 4, 128, NST).transpose(2, 0, 1, 3)),
            "D_l": np.ascontiguousarray(ii["L0_attn_a_D"][:, sl].reshape(2, 4, 128).transpose(2, 0, 1).reshape(128, 8)),
            "out_wT_l": np.ascontiguousarray(
                ii["L0_attn_a_out_w"][:, sl, :].transpose(0, 2, 1)),
            "proj_w_d": np.ascontiguousarray(
                ii["L0_attn_a_proj_w"].reshape(2, D, D)),
            "proj_b": ii["L0_attn_a_proj_b"].reshape(1, D),
            "Wq": ii["L0_attn_b_Wq"],
            "bq": ii["L0_attn_b_bq"].reshape(1, D),
            "Wdown": ii["L0_attn_b_Wdown"],
            "bdown": ii["L0_attn_b_bdown"].reshape(1, DLAT),
            "Wup_k": np.ascontiguousarray(Wup[:, :, 0].reshape(DLAT, D)),
            "Wup_v": np.ascontiguousarray(Wup[:, :, 1].reshape(DLAT, D)),
            "bup_k": np.ascontiguousarray(bup[:, 0].reshape(1, D)),
            "bup_v": np.ascontiguousarray(bup[:, 1].reshape(1, D)),
            "Wout": ii["L0_attn_b_Wout"],
            "bout": ii["L0_attn_b_bout"].reshape(1, D),
            "centT": np.ascontiguousarray(ii["L0_ffn_centroids"].T),
            "eb": ii["L0_ffn_expert_biases"].reshape(1, NE),
            "WrT": np.ascontiguousarray(ii["L0_ffn_Wr"].transpose(0, 2, 1)),
            "br": ii["L0_ffn_br"],
            "WsT": np.ascontiguousarray(ii["L0_ffn_Ws"].transpose(0, 2, 1)),
            "bs": ii["L0_ffn_bs"],
            "identity": np.eye(128, dtype=np.float32),
            "ropeq_tok": np.ascontiguousarray(ropek[q * TR:(q + 1) * TR]),
            "bdmask": bdmask,
        }
        in_maps.append(m)

    nc = build_program(dbg)
    res = run_bass_kernel_spmd(nc, in_maps, core_ids=list(range(8)))
    out = np.empty((2, T, D), np.float32)
    for c in range(8):
        b, q = divmod(c, 4)
        out[b, q * TR:(q + 1) * TR] = res.results[c]["out"]
    if dbg:
        kernel._dbg = [
            np.stack([np.concatenate([res.results[b * 4 + q][f"dbg{i}"]
                                      for q in range(4)], 0) for b in range(2)])
            for i in range(4)]
    return out
